# version 13
# baseline (speedup 1.0000x reference)
import numpy as np
from contextlib import ExitStack

import concourse.bass as bass
import concourse.mybir as mybir
from concourse.ap import AP
from concourse.bass_utils import run_bass_kernel_spmd

F32 = mybir.dt.float32
BF16 = mybir.dt.bfloat16
AF = mybir.ActivationFunctionType
ALU = mybir.AluOpType
AXX = mybir.AxisListType.X

D = 1024
S = 2048
NB = 2
NH = 16
DK = 64
DFF = 2816
NE = 8
KB = S // 128
TB = S // 128
EPS = 1e-5
SCALE_A = 1.0 / float(np.sqrt(DK))
R_SOFTPLUS_0 = 1.442695041

STEP = {"exec": 1, "dma": 16}


def ts(i, n):
    return slice(i * n, (i + 1) * n)



class Buf:
    __slots__ = ("w", "r", "name")

    def __init__(self, name, nslots=1):
        self.name = name
        self.w = [None] * nslots
        self.r = [[] for _ in range(nslots)]


class Stream:
    def __init__(self, name, section, kind):
        self.name = name
        self.section = section
        self.kind = kind
        self.ops = []
        self.sem = None
        self.rank = None


class Sched:
    def __init__(self, nc):
        self.nc = nc
        self.streams = {}
        for name, section, kind in (
            ("T", "tensor", "exec"),
            ("V", "vector", "exec"),
            ("S", "scalar", "exec"),
            ("Y", "sync", "dma"),
            ("G", "gpsimd", "dma"),
        ):
            self.streams[name] = Stream(name, section, kind)

    def op(self, sname, fn, reads=(), writes=(), force_inc=False):
        st = self.streams[sname]
        seq = len(st.ops)
        waits = {}

        def add_wait(tok):
            if tok is None:
                return
            tsid, tseq = tok
            if tsid == sname and sname == "T":
                return
            if self.streams[tsid].kind == "dma":
                key = (tsid, tseq % Sched.DMA_POOL)
            else:
                key = tsid
            if waits.get(key, -1) < tseq:
                waits[key] = tseq

        def norm(e):
            return e if isinstance(e, tuple) else (e, 0)

        for e in reads:
            b, sl = norm(e)
            add_wait(b.w[sl])
        for e in writes:
            b, sl = norm(e)
            add_wait(b.w[sl])
            for t in b.r[sl]:
                add_wait(t)
        st.ops.append([fn, waits, force_inc or st.kind == "dma"])
        tok = (sname, seq)
        for e in reads:
            b, sl = norm(e)
            b.r[sl].append(tok)
        for e in writes:
            b, sl = norm(e)
            b.w[sl] = tok
            b.r[sl] = []
        for key, tseq in waits.items():
            tsid = key[0] if isinstance(key, tuple) else key
            self.streams[tsid].ops[tseq][2] = True
        return tok

    @staticmethod
    def retire(bufs):
        toks = []
        for b in bufs:
            for sl in range(len(b.w)):
                if b.w[sl] is not None:
                    toks.append(b.w[sl])
                toks.extend(b.r[sl])
        return toks

    @staticmethod
    def fence_into(newbufs, tokens):
        for b in newbufs:
            for sl in range(len(b.w)):
                b.r[sl] = list(tokens) + b.r[sl]

    DMA_POOL = 16

    def emit(self, es):
        K = Sched.DMA_POOL
        for st in self.streams.values():
            if st.kind == "dma":
                st.sem = [es.enter_context(self.nc.semaphore(f"sem_{st.name}{i}"))
                          for i in range(K)]
            else:
                st.sem = es.enter_context(self.nc.semaphore("sem_" + st.name))
                st.rank = []
                c = 0
                for rec in st.ops:
                    if rec[2]:
                        c += 1
                    st.rank.append(c)
        blk = es.enter_context(self.nc.Block())
        sched = self

        def wait_for(eng, hw, tsid, tseq):
            tst = sched.streams[tsid]
            if tst.kind == "dma":
                si = tseq % K
                val = (tseq // K + 1) * 16
                key = (tsid, si)
                sem = tst.sem[si]
            else:
                val = tst.rank[tseq] * 1
                key = tsid
                sem = tst.sem
            if hw.get(key, -1) >= val:
                return
            hw[key] = val
            eng.wait_ge(sem, val)

        def make_section(sn):
            st = sched.streams[sn]

            def body(eng):
                hw = {}
                for seq, rec in enumerate(st.ops):
                    fn, waits, inc = rec
                    if st.kind == "dma" and seq >= K:
                        wait_for(eng, hw, sn, seq - K)
                    for key in sorted(waits, key=str):
                        tsid = key[0] if isinstance(key, tuple) else key
                        wait_for(eng, hw, tsid, waits[key])
                    ins = fn(eng)
                    if st.kind == "dma":
                        ins.then_inc(st.sem[seq % K], 16)
                    elif inc:
                        ins.then_inc(st.sem, 1)

            return body

        blk.tensor(make_section("T"))
        blk.vector(make_section("V"))
        blk.scalar(make_section("S"))
        blk.sync(make_section("Y"))
        blk.gpsimd(make_section("G"))



class Arena:

    def __init__(self, nc, base_addr, size):
        self.nc = nc
        self.base = base_addr
        self.size = size
        self.off = 0
        self.n = 0

    def place(self, name, shape, dtype, at=None):
        per_part = int(np.prod(shape[1:])) * mybir.dt.size(dtype)
        if at is None:
            at = (self.off + 31) & ~31
            self.off = at + per_part
        assert at + per_part <= self.size, (
            f"arena overflow: {name} at {at}+{per_part} > {self.size}")
        self.n += 1
        h = self.nc.virtual_sbuf_tensor(f"{name}_{self.n}", list(shape), dtype)
        self.nc._tensor(h.name, list(h.shape), h.dtype, type="SB")
        m = self.nc.lookup_mloc(h)
        m.allocated = True
        m.addr = self.base + at
        return h, at



def build_program(n_sub=4):
    nc = bass.Bass("TRN2", target_bir_lowering=False)

    x_in = nc.dram_tensor("x", [NB, S, D], F32, kind="ExternalInput")
    aWq = nc.dram_tensor("aWq", [D, D], F32, kind="ExternalInput")
    aWk = nc.dram_tensor("aWk", [D, D], F32, kind="ExternalInput")
    aWv = nc.dram_tensor("aWv", [D, D], F32, kind="ExternalInput")
    aWo = nc.dram_tensor("aWo", [D, D], F32, kind="ExternalInput")
    abq = nc.dram_tensor("abq", [128, 8], F32, kind="ExternalInput")
    abk = nc.dram_tensor("abk", [128, 8], F32, kind="ExternalInput")
    abv = nc.dram_tensor("abv", [D], F32, kind="ExternalInput")
    abo = nc.dram_tensor("abo", [D], F32, kind="ExternalInput")
    arel = nc.dram_tensor("arel", [1023, 16], F32, kind="ExternalInput")
    bscal = nc.dram_tensor("bscal", [64], F32, kind="ExternalInput")
    bqkvw = nc.dram_tensor("bqkvw", [1536, D], F32, kind="ExternalInput")
    bqkvb = nc.dram_tensor("bqkvb", [1536], F32, kind="ExternalInput")
    bqkvbT = nc.dram_tensor("bqkvbT", [128, 12], F32, kind="ExternalInput")
    bow = nc.dram_tensor("bow", [D, D], F32, kind="ExternalInput")
    bob = nc.dram_tensor("bob", [D], F32, kind="ExternalInput")
    fw1 = nc.dram_tensor("fw1", [NE, D, DFF], F32, kind="ExternalInput")
    fw3 = nc.dram_tensor("fw3", [NE, D, DFF], F32, kind="ExternalInput")
    fw2 = nc.dram_tensor("fw2", [NE, DFF, D], F32, kind="ExternalInput")
    c_irev = nc.dram_tensor("c_irev", [128, 128], F32, kind="ExternalInput")
    c_sel = nc.dram_tensor("c_sel", [2, 128, 128], F32, kind="ExternalInput")

    x_out = nc.dram_tensor("out", [NB, S, D], F32, kind="ExternalOutput")

    xA = nc.dram_tensor("xA", [NB, S, D], F32)
    xB = nc.dram_tensor("xB", [NB, S, D], F32)
    lnx_dram = nc.dram_tensor("lnx_dram", [NB, S, D], BF16)
    ext_dram = nc.dram_tensor("ext_dram", [NH, 4096], BF16)
    den_dram = nc.dram_tensor("den_dram", [2, S], F32)
    rec_dram = nc.dram_tensor("rec_dram", [2, S], F32)
    ot_dram = nc.dram_tensor("ot_dram", [2, 64, S], BF16)
    wTb_dram = nc.dram_tensor("wTb_dram", [1536, D], BF16)
    kv_dram = nc.dram_tensor("kv_dram", [128, 2 * S], BF16)
    wTo_dram = nc.dram_tensor("wTo_dram", [D, D], BF16)
    rep_dram = nc.dram_tensor("rep_dram", [NH, 128, 3968], BF16)
    w2bf_dram = nc.dram_tensor("w2bf_dram", [NE, DFF, D], BF16)
    w1bf_dram = nc.dram_tensor("w1bf_dram", [NE, D, DFF], BF16)
    w3bf_dram = nc.dram_tensor("w3bf_dram", [NE, D, DFF], BF16)

    es = ExitStack()
    with es:
        S_ = Sched(nc)
        BUF = {}

        def buf(name, nslots=1):
            if name not in BUF:
                BUF[name] = Buf(name, nslots)
            return BUF[name]

        op = S_.op

        def dma(q, out, in_, reads=(), writes=(), force_inc=False):
            return op(q, lambda e, out=out, in_=in_: e.dma_start(out=out, in_=in_),
                      reads=reads, writes=writes, force_inc=force_inc)

        def dmaT(q, out, in_, reads=(), writes=()):
            return op(q, lambda e, out=out, in_=in_:
                      e.dma_start_transpose(out=out, in_=in_),
                      reads=reads, writes=writes)

        ARENA = 178688
        arena_h = nc.alloc_sbuf_tensor("arena", [128, ARENA], mybir.dt.uint8)
        arena_base = nc.lookup_mloc(arena_h).addr

        sb_p = lambda n, sh, dt: es.enter_context(nc.sbuf_tensor(n, sh, dt))
        irev = sb_p("irev", [128, 128], BF16)
        selb = sb_p("selb", [128, 256], BF16)
        biases = sb_p("biases", [128, 28], F32)
        brow = sb_p("brow", [1, 3 * D + 256], BF16)
        scol = sb_p("scol", [128, 8], F32)
        bnd = sb_p("bnd", [128, 36], F32)
        mv = sb_p("mv", [128, 4], F32)
        stats = sb_p("stats", [128, 16], F32)
        rsq = sb_p("rsq", [128, 64], F32)
        epsc = sb_p("epsc", [128, 1], F32)
        w2cv = sb_p("w2cv", [128, 2048], BF16)
        x_tile = sb_p("x_tile", [128, 2 * D], F32)
        o_tile = sb_p("o_tile", [128, 2 * D], F32)
        lnx_tile = sb_p("lnx_tile", [128, 2 * D], BF16)

        p_sc0 = es.enter_context(nc.psum_tensor("p_sc0", [128, 1024], F32))
        p_sc1 = es.enter_context(nc.psum_tensor("p_sc1", [128, 1024], F32))
        p_ot = es.enter_context(nc.psum_tensor("p_ot", [128, 2048], F32))
        b_psc = [buf("p_sc0", 2), buf("p_sc1", 2)]
        b_pot = buf("p_ot", 2)
        PSC = (p_sc0, p_sc1)
        WHOLE0 = [(b_psc[0], 0), (b_psc[0], 1)]
        WHOLE1 = [(b_psc[1], 0), (b_psc[1], 1)]
        WHOLEO = [(b_pot, 0), (b_pot, 1)]

        b_const = buf("const")
        b_small = buf("small")
        b_ext = buf("ext_dram")
        b_lnd = [buf(("lnxd", b)) for b in range(NB)]
        b_w2cv = buf("w2cv")
        b_lnxm = buf("lnx_tile", 2)
        b_xt = buf("x_tile", 2)
        b_ot_t = buf("o_tile", 2)
        b_stats = buf("stats_mv")
        b_bnd = buf("bnd")
        b_rsq = buf("rsq", 2)
        b_dend = buf("den_dram", 2)
        b_recd = buf("rec_dram", 2)
        b_otd = buf("ot_dram", 2)
        b_w2bf = [buf(("w2bf", e)) for e in range(NE)]
        b_w13bf = [[buf(("w13bf", m, e)) for e in range(NE)] for m in range(2)]

        def xdram_buf(t, b):
            return buf(("xd", t.name, b))

        ph0 = Arena(nc, arena_base, ARENA)
        flat_rel, _ = ph0.place("flat_rel", [1, 16368], F32)
        ext_row, _ = ph0.place("ext_row", [1, 4096], F32)
        ext_bfr, _ = ph0.place("ext_bfr", [1, 4096], BF16)
        b_flat = buf("flat_rel")
        b_extr = buf("ext_row")
        b_extb = buf("ext_bfr")

        dma("G", irev[:, :], c_irev[:, :], writes=[b_const])
        op("V", lambda e: e.memset(epsc[:, :], EPS), writes=[b_small])
        dma("G", selb[:, :].rearrange("p (s m) -> p s m", s=2),
            c_sel[:, :, :].rearrange("s p m -> p s m"), writes=[b_const])
        dma("Y", biases[:, 0:8], abq[:, :], writes=[b_small])
        dma("Y", biases[:, 8:16], abk[:, :], writes=[b_small])
        dma("Y", biases[:, 16:28], bqkvbT[:, :], writes=[b_small])
        dma("G", brow[0:1, 0:D], abo[:].unsqueeze(0), writes=[b_small])
        dma("G", brow[0:1, D:2 * D], bob[:].unsqueeze(0), writes=[b_small])
        dma("G", brow[0:1, 2 * D:3 * D], abv[:].unsqueeze(0),
            writes=[b_small])
        dma("G", brow[0:1, 3 * D:3 * D + 256],
            bqkvb[1280:1536].unsqueeze(0), writes=[b_small])
        dma("Y", scol[:, 0:1], AP(bscal, 0, [[0, 2], [1, 64]]), writes=[b_small])
        op("S", lambda e: e.activation(scol[:, 1:2], scol[:, 0:1], AF.Abs),
           reads=[b_small], writes=[b_small])
        op("S", lambda e: e.activation(scol[:, 2:3], scol[:, 1:2], AF.Exp,
                                       scale=-1.0),
           reads=[b_small], writes=[b_small])
        op("S", lambda e: e.activation(scol[:, 2:3], scol[:, 2:3], AF.Ln, bias=1.0),
           reads=[b_small], writes=[b_small])
        op("V", lambda e: e.tensor_scalar_max(scol[:, 3:4], scol[:, 0:1], 0.0),
           reads=[b_small], writes=[b_small])
        op("V", lambda e: e.tensor_add(scol[:, 3:4], scol[:, 3:4], scol[:, 2:3]),
           reads=[b_small], writes=[b_small])
        op("V", lambda e: e.tensor_scalar_mul(
            scol[:, 3:4], scol[:, 3:4], float(R_SOFTPLUS_0 / np.sqrt(DK))),
           reads=[b_small], writes=[b_small])

        dma("Y", flat_rel[0:1, :], arel[:, :].rearrange("a b -> (a b)").unsqueeze(0),
            writes=[b_flat])
        for h in range(NH):
            op("V", lambda e, h=h: e.tensor_copy(
                ext_row[0:1, 0:1023],
                AP(flat_rel.tensor if hasattr(flat_rel, "tensor") else flat_rel,
                   h, [[16368, 1], [16, 1023]])
                if False else
                AP(flat_rel, h, [[16368, 1], [16, 1023]])),
               reads=[b_flat], writes=[b_extr])
            op("V", lambda e, h=h: e.tensor_copy(
                ext_row[0:1, 1023:4096],
                AP(flat_rel, 1022 * 16 + h, [[16368, 1], [0, 3073]])),
               reads=[b_flat], writes=[b_extr])
            op("S", lambda e: e.activation(ext_bfr[0:1, :], ext_row[0:1, :],
                                           AF.Exp),
               reads=[b_extr], writes=[b_extb])
            dma("Y", ext_dram[h, :].unsqueeze(0), ext_bfr[0:1, :],
                reads=[b_extb], writes=[b_ext])

        srep0, _ = ph0.place("srep0", [128, 3968], BF16)
        rtab0, _ = ph0.place("rtab0", [128, 3968], BF16)
        b_srep0 = buf("srep0")
        b_rtab0 = buf("rtab0")
        for h in range(NH):
            dma("G", srep0[:, 0:3968],
                AP(ext_dram, h * 4096, [[1, 128], [1, 3968]]),
                reads=[b_ext], writes=[b_srep0])
            for c8 in range(8):
                n0 = c8 * 496
                op("T", lambda e, n0=n0: e.matmul(
                    p_sc0[:, 0:496], irev[:, :], srep0[:, n0:n0 + 496],
                    start=True, stop=True),
                   reads=[b_srep0, b_const], writes=WHOLE0)
                op("V", lambda e, n0=n0: e.tensor_copy(
                    rtab0[:, n0:n0 + 496], p_sc0[:, 0:496]),
                   reads=WHOLE0, writes=[b_rtab0])
            dma("Y", rep_dram[h, :, :], rtab0[:, :],
                reads=[b_rtab0], writes=[buf("rep_dram")])

        for r0 in range(0, 1536, 256):
            dma("G", w2cv[:, 0:2048].rearrange("p (c m) -> p c m", m=D),
                bqkvw[r0:r0 + 256, :].rearrange("(c p) m -> p c m", p=128),
                writes=[b_w2cv])
            dma("G", wTb_dram[r0:r0 + 256, :].rearrange("(c p) m -> p c m", p=128),
                w2cv[:, 0:2048].rearrange("p (c m) -> p c m", m=D),
                reads=[b_w2cv], writes=[buf("wTb_dram")])
        for r0 in range(0, D, 256):
            dma("G", w2cv[:, 0:2048].rearrange("p (c m) -> p c m", m=D),
                bow[r0:r0 + 256, :].rearrange("(c p) m -> p c m", p=128),
                writes=[b_w2cv])
            dma("G", wTo_dram[r0:r0 + 256, :].rearrange("(c p) m -> p c m", p=128),
                w2cv[:, 0:2048].rearrange("p (c m) -> p c m", m=D),
                reads=[b_w2cv], writes=[buf("wTo_dram")])

        phase0_bufs = [b_flat, b_extr, b_extb, b_srep0, b_rtab0]

        def emit_ln(xsrc, b):
            for tb in range(TB):
                sl = tb % 2
                xt = x_tile[:, ts(sl, D)]
                dma("Y", xt, xsrc[b, ts(tb, 128), :],
                    reads=[xdram_buf(xsrc, b)], writes=[(b_xt, sl)])
                op("V", lambda e, xt=xt: e.bn_stats(stats[:, 0:6], xt[:, 0:512]),
                   reads=[(b_xt, sl)], writes=[b_stats])
                op("V", lambda e, xt=xt: e.bn_stats(stats[:, 6:12], xt[:, 512:1024]),
                   reads=[(b_xt, sl)], writes=[b_stats])
                op("V", lambda e: e.bn_aggr(mv[:, 0:2], stats[:, 0:12]),
                   reads=[b_stats], writes=[b_stats])
                op("S", lambda e: e.activation(mv[:, 2:3], mv[:, 1:2], AF.Ln,
                                               bias=epsc[:, 0:1]),
                   reads=[b_stats], writes=[b_stats])
                op("S", lambda e: e.activation(mv[:, 2:3], mv[:, 2:3], AF.Exp,
                                               scale=-0.5),
                   reads=[b_stats], writes=[b_stats])
                op("V", lambda e, xt=xt, sl=sl: e.tensor_scalar(
                    lnx_tile[:, ts(sl, D)], xt, mv[:, 0:1], mv[:, 2:3],
                    ALU.subtract, ALU.mult),
                   reads=[(b_xt, sl), b_stats], writes=[(b_lnxm, sl)])
                dma("Y", lnx_dram[b, ts(tb, 128), :], lnx_tile[:, ts(sl, D)],
                    reads=[(b_lnxm, sl)], writes=[b_lnd[b]])

        def emit_lnxT(dst, bdst, b):
            for c in range(8):
                dmaT("Y", dst[:, ts(c, S)], lnx_dram[b, :, ts(c, 128)],
                     reads=[b_lnd[b]], writes=[bdst])

        def emit_oproj(xsrc, xdst, otbuf, b_otb, wmov, b_wmov, bias_row, b_i):
            for tb in range(TB):
                sl = tb % 2
                ph = p_ot[:, ts(sl, 1024)]
                xt = x_tile[:, ts(sl, D)]
                dma("Y", xt, xsrc[b_i, ts(tb, 128), :],
                    reads=[xdram_buf(xsrc, b_i)], writes=[(b_xt, sl)])
                for n2 in range(2):
                    op("T", lambda e, ph=ph, n2=n2, bias_row=bias_row:
                       e.matmul(ph[:, ts(n2, 512)], selb[0:1, 0:128],
                                bias_row[0:1, ts(n2, 512)], start=True, stop=False),
                       reads=[b_const, b_small], writes=[(b_pot, sl)])
                for c in range(8):
                    for n2 in range(2):
                        op("T", lambda e, ph=ph, c=c, n2=n2, tb=tb, otbuf=otbuf,
                           wmov=wmov:
                           e.matmul(ph[:, ts(n2, 512)],
                                    otbuf[:, c * S + tb * 128:c * S + tb * 128 + 128],
                                    wmov[:, c * D + n2 * 512:c * D + n2 * 512 + 512],
                                    start=False, stop=(c == 7)),
                           reads=[b_otb, b_wmov], writes=[(b_pot, sl)])
                ot = o_tile[:, ts(sl, D)]
                op("V", lambda e, ot=ot, ph=ph, xt=xt: e.tensor_add(ot, ph, xt),
                   reads=[(b_pot, sl), (b_xt, sl)], writes=[(b_ot_t, sl)])
                dma("Y", xdst[b_i, ts(tb, 128), :], ot,
                    reads=[(b_ot_t, sl)], writes=[xdram_buf(xdst, b_i)],
                    force_inc=(xdst is x_out))

        PSC3 = (p_sc0[:, 0:1024], p_sc1[:, 0:1024], p_ot[:, 0:1024])
        B3 = (WHOLE0, WHOLE1, [(b_pot, 0)])
        p_pd = p_ot[:, 0:1024]
        B_PD = [(b_pot, 0)]
        p_otr = p_ot[:, 1024:2048]
        B_OTR = [(b_pot, 1)]

        def emit_attention(kind, xsrc, xdst):
            is_a = kind == "a"
            vw = (NH if is_a else 4) * 65
            ar = Arena(nc, arena_base, ARENA)
            lnxT, _ = ar.place("lnxT" + kind, [128, 8 * S], BF16)
            otbuf, _ = ar.place("otbuf" + kind, [128, 8 * S], BF16)
            vtok, _ = ar.place("vtok" + kind, [128, KB * vw], BF16)
            qkT, _ = ar.place("qkT" + kind, [128, 4 * S], BF16)
            probs, pr_at = ar.place("probs" + kind, [128, 3 * 1024], BF16)
            sqt, _ = ar.place("sqt" + kind, [128, S], BF16, at=pr_at)
            otraw, _ = ar.place("otraw" + kind, [65, S], BF16)
            denrep, dr_at = ar.place("denrep" + kind, [65, 2 * 1024], F32)
            if is_a:
                wpair, _ = ar.place("wpair_a", [128, 2 * 2 * 1024], BF16)
                rep2e, re_at = ar.place("rep2e_a", [128, 4 * 3968], BF16)
                wmov, _ = ar.place("wmov_a", [128, 8 * D], BF16, at=re_at)
                wv, _ = ar.place("wv_a", [128, 8 * D], BF16,
                                 at=re_at + 16 * 1024)
                wqkvT = kvT = kvTalt = None
            else:
                wqkvT, _ = ar.place("wqkvT", [128, 8 * 1536], BF16)
                kvT, _ = ar.place("kvT", [128, 2 * S], BF16)
                kvTalt, _ = ar.place("kvTalt", [128, 2 * S], BF16)
                wmov, _ = ar.place("wmov_b", [128, 8 * D], BF16)
                wpair = rep2e = wv = None

            nb = []

            def mk(name, nslots=1):
                b = Buf((kind, name), nslots)
                nb.append(b)
                return b

            b_lnxT = mk("lnxT")
            b_otb = mk("otbuf")
            b_vt = mk("vtok")
            b_qk = mk("qkT", 2)
            b_pr = mk("probs", 3)
            b_sq = mk("sqt")
            b_otr = mk("otraw")
            b_dr = mk("denrep", 2)
            b_wp = mk("wpair", 2)
            b_rep = mk("rep2e", 2)
            b_wmov = mk("wmov")
            b_kvT = mk("kvT")
            b_kvTa = mk("kvTalt")
            b_bd = mk("bnd", 2)
            b_bkv = mk("bndkv")
            b_kvd = buf("kv_dram")
            if is_a:
                b_wmov = b_rep
            yield nb

            emit_ln(xsrc, 0)
            emit_ln(xsrc, 1)

            if not is_a:
                for c in range(8):
                    dmaT("Y", wqkvT[:, ts(c, 1536)], wTb_dram[:, ts(c, 128)],
                         reads=[buf("wTb_dram")], writes=[b_wp])
                for c in range(8):
                    dmaT("Y", wmov[:, ts(c, D)], wTo_dram[:, ts(c, 128)],
                         reads=[buf("wTo_dram")], writes=[b_wmov])

            def proj_chunk_pd(ncc, dst_ap, b_dst, lhs_fn, lhs_reads, bias_col,
                              scale_v):
                half = ncc % 2
                pd = p_pd[:, ts(half, 512)]
                for c in range(8):
                    op("T", lambda e, c=c, ncc=ncc, pd=pd, lhs_fn=lhs_fn:
                       e.matmul(pd, lhs_fn(c),
                                lnxT[:, c * S + ncc * 512:c * S + ncc * 512 + 512],
                                start=(c == 0), stop=(c == 7)),
                       reads=list(lhs_reads) + [b_lnxT], writes=B_PD)
                if scale_v is None:
                    op("V", lambda e, ncc=ncc, pd=pd: e.tensor_scalar(
                        dst_ap[:, ts(ncc, 512)], pd, bias_col, None, ALU.add),
                       reads=B_PD + [b_small], writes=[b_dst])
                else:
                    op("V", lambda e, ncc=ncc, pd=pd: e.tensor_scalar(
                        dst_ap[:, ts(ncc, 512)], pd, bias_col, scale_v,
                        ALU.add, ALU.mult),
                       reads=B_PD + [b_small], writes=[b_dst])

            def bound_head(src_ap, b_src, half, col, b_b):
                op("V", lambda e: e.tensor_mul(sqt[:, :], src_ap, src_ap),
                   reads=[b_src], writes=[b_sq, (b_pr, 0), (b_pr, 1)])
                base12 = (col // 12) * 12
                for i2 in range(2):
                    for j2 in range(2):
                        op("T", lambda e, i2=i2, j2=j2:
                           e.matmul(p_pd[:, ts(j2, 512)], selb[:, ts(half, 128)],
                                    sqt[:, ts(2 * i2 + j2, 512)],
                                    start=True, stop=True),
                           reads=[b_sq, (b_pr, 0), (b_pr, 1), b_const],
                           writes=B_PD)
                    op("V", lambda e, i2=i2, base12=base12: e.tensor_reduce(
                        bnd[:, base12 + 8 + i2:base12 + 9 + i2],
                        p_pd[:, 0:1024], AXX, ALU.max),
                       reads=B_PD, writes=[b_b])
                op("V", lambda e, col=col, base12=base12: e.tensor_max(
                    bnd[:, col:col + 1], bnd[:, base12 + 8:base12 + 9],
                    bnd[:, base12 + 9:base12 + 10]),
                   reads=[b_b], writes=[b_b])

            def finish_bound(qcol, kcol, outcol, scale, b_b, extra_reads=()):
                tcol = (outcol // 12) * 12 + 10
                op("V", lambda e: e.tensor_mul(
                    bnd[:, tcol:tcol + 1], bnd[:, qcol:qcol + 1],
                    bnd[:, kcol:kcol + 1]),
                   reads=[b_b] + list(extra_reads), writes=[b_b])
                op("S", lambda e: e.activation(bnd[:, tcol:tcol + 1],
                                               bnd[:, tcol:tcol + 1], AF.Ln),
                   reads=[b_b], writes=[b_b])
                op("S", lambda e: e.activation(bnd[:, tcol:tcol + 1],
                                               bnd[:, tcol:tcol + 1], AF.Exp,
                                               scale=0.5),
                   reads=[b_b], writes=[b_b])
                op("V", lambda e: e.tensor_scalar_mul(
                    bnd[:, outcol:outcol + 1], bnd[:, tcol:tcol + 1],
                    -float(scale)),
                   reads=[b_b], writes=[b_b])

            exp_scale = SCALE_A if is_a else 1.0

            pending = [None, None]

            def flush_pending(st_):
                if pending[st_] is not None:
                    pending[st_]()
                    pending[st_] = None

            def prep_steps(pair, b_i):
                py = pair % 2
                qk0 = 2 * S * py
                if is_a:
                    for mi, wm in enumerate((aWq, aWk)):
                        dma("G",
                            wpair[:, (py * 2 + mi) * D:(py * 2 + mi + 1) * D]
                            .rearrange("p (c m) -> p c m", m=128),
                            wm[:, ts(pair, 128)].rearrange("(c p) m -> p c m",
                                                           p=128),
                            writes=[(b_wp, py)])
                    for hh in range(2):
                        h = 2 * pair + hh
                        dma("G", rep2e[:, (py * 2 + hh) * 3968:
                                       (py * 2 + hh + 1) * 3968],
                            rep_dram[h, :, :],
                            reads=[buf("rep_dram")], writes=[(b_rep, py)])
                    yield
                    for ncc in range(4):
                        proj_chunk_pd(ncc, qkT[:, qk0:qk0 + S], (b_qk, py),
                                      lambda c, py=py:
                                      wpair[:, (py * 2) * D + c * 128:
                                            (py * 2) * D + c * 128 + 128],
                                      [(b_wp, py)], biases[:, pair:pair + 1],
                                      None)
                        yield
                    for ncc in range(4):
                        proj_chunk_pd(ncc, qkT[:, qk0 + S:qk0 + 2 * S],
                                      (b_qk, py),
                                      lambda c, py=py:
                                      wpair[:, (py * 2 + 1) * D + c * 128:
                                            (py * 2 + 1) * D + c * 128 + 128],
                                      [(b_wp, py)], biases[:, 8 + pair:9 + pair],
                                      None)
                        yield
                    for hh in range(2):
                        bound_head(qkT[:, qk0:qk0 + S], (b_qk, py), hh,
                                   py * 12 + 0 + hh, (b_bd, py))
                        yield
                        bound_head(qkT[:, qk0 + S:qk0 + 2 * S], (b_qk, py), hh,
                                   py * 12 + 2 + hh, (b_bd, py))
                        yield
                        finish_bound(py * 12 + 0 + hh, py * 12 + 2 + hh,
                                     py * 12 + 6 + hh, SCALE_A, (b_bd, py))
                        yield
                else:
                    for ncc in range(4):
                        proj_chunk_pd(ncc, qkT[:, qk0:qk0 + S], (b_qk, py),
                                      lambda c, pair=pair:
                                      wqkvT[:, c * 1536 + pair * 128:
                                            c * 1536 + pair * 128 + 128],
                                      [b_wp], biases[:, 16 + pair:17 + pair],
                                      scol[:, 3:4])
                        yield
                    for hh in range(2):
                        g = (2 * pair + hh) // 4
                        bound_head(qkT[:, qk0:qk0 + S], (b_qk, py), hh,
                                   py * 12 + 0 + hh, (b_bd, py))
                        yield
                        finish_bound(py * 12 + 0 + hh, 24 + g,
                                     py * 12 + 6 + hh, 1.0, (b_bd, py),
                                     extra_reads=[b_bkv])
                        yield
                op("V", lambda e: e.tensor_tensor(
                    bnd[:, py * 12 + 11:py * 12 + 12],
                    bnd[:, py * 12 + 6:py * 12 + 7],
                    bnd[:, py * 12 + 7:py * 12 + 8], ALU.min),
                   reads=[(b_bd, py)], writes=[(b_bd, py)])

            def sdpa_steps(pair, b_i):
                py = pair % 2
                qk0 = 2 * S * py
                h0 = 2 * pair
                HH = []
                for hh in range(2):
                    h = h0 + hh
                    if is_a:
                        HH.append((qkT[:, qk0 + S:qk0 + 2 * S], (b_qk, py),
                                   h * 65, (py * 2 + hh) * 3968))
                    else:
                        g = h // 4
                        if (g % 2) == (hh % 2):
                            HH.append((kvT[:, ts(g // 2, S)], b_kvT, g * 65,
                                       None))
                        else:
                            HH.append((kvTalt[:, ts(g // 2, S)], b_kvTa,
                                       g * 65, None))
                negb = bnd[:, py * 12 + 11:py * 12 + 12]

                for q4 in range(4):
                    half = q4 // 2
                    qoff = q4 * 512

                    def emit_sc(kb, q4=q4, qoff=qoff):
                        sl = kb % 3
                        pt = PSC3[sl]
                        pp = B3[sl]
                        for hh in range(2):
                            kT_ap, b_k, voff, repoff = HH[hh]
                            base = 64 * hh
                            op("T", lambda e, pt=pt, kb=kb, hh=hh, base=base,
                               kT_ap=kT_ap, b_k=b_k:
                               e.matmul(
                                   pt[:, ts(hh, 512)],
                                   kT_ap[base:base + 64, ts(kb, 128)],
                                   qkT[base:base + 64, qk0 + qoff:
                                       qk0 + qoff + 512],
                                   start=True, stop=True),
                               reads=[b_k, (b_qk, py)], writes=pp)

                    def emit_exp(kb, q4=q4, qoff=qoff):
                        sl = kb % 3
                        pt = PSC3[sl]
                        pp = B3[sl]
                        op("S", lambda e, pt=pt, sl=sl:
                           e.activation(probs[:, ts(sl, 1024)], pt[:, 0:1024],
                                        AF.Exp, bias=negb,
                                        scale=float(exp_scale)),
                           reads=pp + [(b_bd, py)], writes=[(b_pr, sl)])
                        if is_a:
                            for hh in range(2):
                                repoff = HH[hh][3]
                                off = repoff + 1920 - 128 * kb + qoff
                                op("V", lambda e, sl=sl, off=off, hh=hh:
                                   e.tensor_mul(
                                       probs[:, sl * 1024 + hh * 512:
                                             sl * 1024 + hh * 512 + 512],
                                       probs[:, sl * 1024 + hh * 512:
                                             sl * 1024 + hh * 512 + 512],
                                       rep2e[:, off:off + 512]),
                                   reads=[(b_pr, sl), (b_rep, py)],
                                   writes=[(b_pr, sl)])

                    def emit_pv(kb, q4=q4):
                        sl = kb % 3
                        for hh in range(2):
                            voff = HH[hh][2]
                            op("T", lambda e, kb=kb, sl=sl, hh=hh, voff=voff:
                               e.matmul(
                                   p_otr[0:65, ts(hh, 512)],
                                   vtok[:, kb * vw + voff:kb * vw + voff + 65],
                                   probs[:, sl * 1024 + hh * 512:
                                         sl * 1024 + hh * 512 + 512],
                                   start=(kb == 0), stop=(kb == KB - 1)),
                               reads=[(b_pr, sl), b_vt], writes=B_OTR)

                    emit_sc(0)
                    emit_sc(1)
                    for kb in range(KB):
                        emit_exp(kb)
                        if kb + 2 < KB:
                            emit_sc(kb + 2)
                        emit_pv(kb)
                        if kb % 2 == 1:
                            yield False

                    for hh in range(2):
                        even = hh == 0
                        if q4 % 2 == 0:
                            flush_pending(hh)
                        if even:
                            dst_q = otbuf[0:64, pair * S + qoff:
                                          pair * S + qoff + 512]
                            b_dst = b_otb
                        else:
                            dst_q = otraw[0:64, qoff:qoff + 512]
                            b_dst = b_otr
                        op("V", lambda e, dst_q=dst_q, hh=hh: e.tensor_copy(
                            dst_q, p_otr[0:64, ts(hh, 512)]),
                           reads=B_OTR, writes=[b_dst])
                        op("V", lambda e, hh=hh, q4=q4: e.tensor_copy(
                            denrep[64:65, hh * 1024 + (q4 % 2) * 512:
                                   hh * 1024 + (q4 % 2) * 512 + 512],
                            p_otr[64:65, ts(hh, 512)]),
                           reads=B_OTR, writes=[(b_dr, hh)])
                        if q4 % 2 == 1:
                            op_half = half
                            dma("G", den_dram[hh, ts(op_half, 1024)].unsqueeze(0),
                                denrep[64:65, ts(hh, 1024)],
                                reads=[(b_dr, hh)], writes=[(b_dend, hh)])
                            dma("G", rsq[:, hh * 32 + op_half * 8:
                                         hh * 32 + op_half * 8 + 8],
                                den_dram[hh, ts(op_half, 1024)]
                                .rearrange("(p c) -> p c", c=8),
                                reads=[(b_dend, hh)], writes=[(b_rsq, hh)])
                            op("V", lambda e, hh=hh, op_half=op_half:
                               e.reciprocal(
                                   rsq[:, hh * 32 + 16 + op_half * 8:
                                       hh * 32 + 16 + op_half * 8 + 8],
                                   rsq[:, hh * 32 + op_half * 8:
                                       hh * 32 + op_half * 8 + 8]),
                               reads=[(b_rsq, hh)], writes=[(b_rsq, hh)])
                            dma("G", rec_dram[hh, ts(op_half, 1024)]
                                .rearrange("(p c) -> p c", c=8),
                                rsq[:, hh * 32 + 16 + op_half * 8:
                                    hh * 32 + 16 + op_half * 8 + 8],
                                reads=[(b_rsq, hh)], writes=[(b_recd, hh)])
                            dma("G", denrep[0:64, ts(hh, 1024)],
                                AP(rec_dram, hh * S + op_half * 1024,
                                   [[0, 64], [1, 1024]]),
                                reads=[(b_recd, hh)], writes=[(b_dr, hh)])

                            def fin(hh=hh, op_half=op_half, even=even,
                                    pair=pair):
                                if even:
                                    dst_h = otbuf[0:64,
                                                  pair * S + op_half * 1024:
                                                  pair * S + op_half * 1024
                                                  + 1024]
                                    b_dsth = b_otb
                                else:
                                    dst_h = otraw[0:64, ts(op_half, 1024)]
                                    b_dsth = b_otr
                                op("V", lambda e: e.tensor_mul(
                                    dst_h, dst_h, denrep[0:64, ts(hh, 1024)]),
                                   reads=[(b_dr, hh), b_dsth], writes=[b_dsth])
                                if not even:
                                    dma("Y", ot_dram[hh, :, ts(op_half, 1024)],
                                        otraw[0:64, ts(op_half, 1024)],
                                        reads=[b_dsth], writes=[(b_otd, hh)])
                                    dma("Y", otbuf[64:128,
                                                   pair * S + op_half * 1024:
                                                   pair * S + op_half * 1024
                                                   + 1024],
                                        ot_dram[hh, :, ts(op_half, 1024)],
                                        reads=[(b_otd, hh)], writes=[b_otb])

                            pending[hh] = fin
                    yield True

            ones_done = [False]
            for b_i in range(NB):
                emit_lnxT(lnxT, b_lnxT, b_i)

                if is_a:
                    dma("G", wv[:, :].rearrange("p (c m) -> p c m", m=D),
                        aWv[:, :].rearrange("(c p) m -> p c m", p=128),
                        writes=[(b_rep, 0), (b_rep, 1)])
                    if not ones_done[0]:
                        op("V", lambda e: e.memset(
                            vtok[:, :].rearrange("p (t h o) -> p t h o",
                                                 h=NH, o=65)
                            [:, :, :, 64:65], 1.0), writes=[b_vt])
                        ones_done[0] = True
                    for tb in range(TB):
                        pp = WHOLE0 if tb % 2 == 0 else WHOLE1
                        pt = PSC[tb % 2]
                        for n2 in range(2):
                            op("T", lambda e, pt=pt, n2=n2: e.matmul(
                                pt[:, ts(n2, 512)], selb[0:1, 0:128],
                                brow[0:1, 2 * D + n2 * 512:2 * D + n2 * 512 + 512],
                                start=True, stop=False),
                               reads=[b_const, b_small], writes=pp)
                            for c in range(8):
                                op("T", lambda e, tb=tb, c=c, n2=n2, pt=pt:
                                   e.matmul(
                                       pt[:, ts(n2, 512)],
                                       lnxT[:, c * S + tb * 128:
                                            c * S + tb * 128 + 128],
                                       wv[:, c * D + n2 * 512:
                                          c * D + n2 * 512 + 512],
                                       start=False, stop=(c == 7)),
                                   reads=[(b_rep, 0), (b_rep, 1), b_lnxT],
                                   writes=pp)
                        op("V", lambda e, tb=tb, pt=pt: e.tensor_copy(
                            vtok[:, tb * vw:(tb + 1) * vw]
                            .rearrange("p (h o) -> p h o", o=65)[:, :, 0:64],
                            pt[:, 0:1024].rearrange("p (h o) -> p h o", o=64)),
                           reads=pp, writes=[b_vt])
                else:
                    for blk2 in range(2):
                        for ncc in range(4):
                            proj_chunk_pd(ncc, kvT[:, ts(blk2, S)], b_kvT,
                                          lambda c, blk2=blk2:
                                          wqkvT[:, c * 1536 + 1024 + blk2 * 128:
                                                c * 1536 + 1024 + blk2 * 128
                                                + 128],
                                          [b_wp],
                                          biases[:, 24 + blk2:25 + blk2], None)
                    dma("Y", kv_dram[:, :], kvT[:, 0:2 * S],
                        reads=[b_kvT], writes=[b_kvd])
                    for blk2 in range(2):
                        dma("Y", kvTalt[0:64, ts(blk2, S)],
                            kv_dram[64:128, ts(blk2, S)],
                            reads=[b_kvd], writes=[b_kvTa])
                        dma("Y", kvTalt[64:128, ts(blk2, S)],
                            kv_dram[0:64, ts(blk2, S)],
                            reads=[b_kvd], writes=[b_kvTa])
                    if not ones_done[0]:
                        op("V", lambda e: e.memset(
                            vtok[:, :].rearrange("p (t h o) -> p t h o",
                                                 h=4, o=65)
                            [:, :, :, 64:65], 1.0), writes=[b_vt])
                        ones_done[0] = True
                    for tb in range(TB):
                        pp = WHOLE0 if tb % 2 == 0 else WHOLE1
                        pt = PSC[tb % 2]
                        op("T", lambda e, pt=pt: e.matmul(
                            pt[:, 0:256], selb[0:1, 0:128],
                            brow[0:1, 3 * D:3 * D + 256], start=True,
                            stop=False),
                           reads=[b_const, b_small], writes=pp)
                        for c in range(8):
                            op("T", lambda e, tb=tb, c=c, pt=pt: e.matmul(
                                pt[:, 0:256],
                                lnxT[:, c * S + tb * 128:c * S + tb * 128 + 128],
                                wqkvT[:, c * 1536 + 1280:c * 1536 + 1536],
                                start=False, stop=(c == 7)),
                               reads=[b_wp, b_lnxT], writes=pp)
                        for g in range(4):
                            op("V", lambda e, tb=tb, g=g, pt=pt: e.tensor_copy(
                                vtok[:, tb * vw + g * 65:tb * vw + g * 65 + 64],
                                pt[:, g * 64:g * 64 + 64]),
                               reads=pp, writes=[b_vt])
                    for g in range(4):
                        bound_head(kvT[:, ts(g // 2, S)], b_kvT, g % 2,
                                   24 + g, b_bkv)

                for _ in prep_steps(0, b_i):
                    pass
                for pair in range(8):
                    g_s = sdpa_steps(pair, b_i)
                    g_p = prep_steps(pair + 1, b_i) if pair < 7 else iter(())
                    for safe in g_s:
                        if safe:
                            for _ in range(4):
                                next(g_p, None)
                    for _ in g_p:
                        pass

                flush_pending(0)
                flush_pending(1)
                if is_a:
                    dma("G", wmov[:, :].rearrange("p (c m) -> p c m", m=D),
                        aWo[:, :].rearrange("(c p) m -> p c m", p=128),
                        writes=[(b_rep, 0), (b_rep, 1)])
                bias_row = brow[0:1, 0:D] if is_a else brow[0:1, D:2 * D]
                emit_oproj(xsrc, xdst, otbuf, b_otb, wmov, b_wmov, bias_row, b_i)

        def emit_ffn(xsrc, xdst, call_i):
            ar = Arena(nc, arena_base, ARENA)
            lnxTf, _ = ar.place("f_lnxT", [128, 8 * 2 * S], BF16)
            h_buf, _ = ar.place("f_h", [128, 22 * 512], BF16)
            w13, _ = ar.place("f_w13", [128, 4 * 8 * 768], BF16)
            w2s, _ = ar.place("f_w2", [128, 2 * 8 * 1024], BF16)
            sg, _ = ar.place("f_sg", [128, 2 * 512], BF16)

            nb = []

            def mk(name, nslots=1):
                b = Buf(("f", name), nslots)
                nb.append(b)
                return b

            b_lnxT = mk("lnxTf", NE)
            b_h = mk("h", 22)
            b_w13 = mk("w13", 4)
            b_w2 = mk("w2", 2)
            b_sg = mk("sg", 2)
            yield nb

            emit_ln(xsrc, 0)
            emit_ln(xsrc, 1)
            for e_i in range(NE):
                for b_i in range(NB):
                    for c in range(8):
                        dmaT("Y", lnxTf[:, c * 2 * S + e_i * 512 + b_i * 256:
                                        c * 2 * S + e_i * 512 + b_i * 256 + 256],
                             lnx_dram[b_i, ts(e_i, 256), ts(c, 128)],
                             reads=[b_lnd[b_i]], writes=[(b_lnxT, e_i)])

            quarters = [(0, 6), (6, 6), (12, 5), (17, 5)]
            w2thirds = [(0, 8), (8, 7), (15, 7)]

            def w13_load(e_i, q_i):
                mb0, mbn = quarters[q_i]
                for mat_i in range(2):
                    wd = (w1bf_dram, w3bf_dram)[mat_i]
                    slot = mat_i * 2 + (e_i * 4 + q_i) % 2
                    view = (w13[:, slot * 8 * 768:slot * 8 * 768 + 8 * mbn * 128]
                            .rearrange("p (c m) -> p c m", m=mbn * 128))
                    dview = (wd[e_i, :, mb0 * 128:(mb0 + mbn) * 128]
                             .rearrange("(c p) m -> p c m", p=128))
                    if call_i == 0:
                        wm = (fw1, fw3)[mat_i]
                        dma("G", view,
                            wm[e_i, :, mb0 * 128:(mb0 + mbn) * 128]
                            .rearrange("(c p) m -> p c m", p=128),
                            writes=[(b_w13, slot)])
                        dma("Y", dview, view,
                            reads=[(b_w13, slot)], writes=[b_w13bf[mat_i][e_i]])
                    else:
                        dma("Y", view, dview,
                            reads=[b_w13bf[mat_i][e_i]], writes=[(b_w13, slot)])

            def w2_load(e_i, t_i):
                kc0, kcn = w2thirds[t_i]
                slot = (e_i * 3 + t_i) % 2
                view = (w2s[:, slot * 8 * D:slot * 8 * D + kcn * D]
                        .rearrange("p (c m) -> p c m", m=D))
                dview = (w2bf_dram[e_i, slice(kc0 * 128, (kc0 + kcn) * 128), :]
                         .rearrange("(c p) m -> p c m", p=128))
                if call_i == 0:
                    dma("G", view,
                        fw2[e_i, slice(kc0 * 128, (kc0 + kcn) * 128), :]
                        .rearrange("(c p) m -> p c m", p=128),
                        writes=[(b_w2, slot)])
                    dma("Y", dview, view,
                        reads=[(b_w2, slot)], writes=[b_w2bf[e_i]])
                else:
                    dma("Y", view, dview,
                        reads=[b_w2bf[e_i]], writes=[(b_w2, slot)])

            for e_i in range(NE):
                for q_i, (mb0, mbn) in enumerate(quarters):
                    w13_load(e_i, q_i)
                    slot1 = 0 * 2 + (e_i * 4 + q_i) % 2
                    slot3 = 1 * 2 + (e_i * 4 + q_i) % 2
                    for mbl in range(mbn):
                        mb = mb0 + mbl
                        gsl = mb % 2
                        for gi, slot in ((0, slot1), (1, slot3)):
                            pt = PSC[gi]
                            pp = (b_psc[gi], gsl)
                            for c in range(8):
                                op("T", lambda e, pt=pt, gsl=gsl, c=c,
                                   slot=slot, mbl=mbl, mbn=mbn, e_i=e_i:
                                   e.matmul(
                                       pt[:, ts(gsl, 512)],
                                       w13[:, slot * 8 * 768 + (c * mbn + mbl)
                                           * 128:
                                           slot * 8 * 768 + (c * mbn + mbl)
                                           * 128 + 128],
                                       lnxTf[:, c * 2 * S + e_i * 512:
                                             c * 2 * S + e_i * 512 + 512],
                                       start=(c == 0), stop=(c == 7)),
                                   reads=[(b_w13, slot), (b_lnxT, e_i)],
                                   writes=[pp])
                        op("S", lambda e, gsl=gsl: e.activation(
                            sg[:, ts(gsl, 512)], p_sc0[:, ts(gsl, 512)], AF.Silu),
                           reads=[(b_psc[0], gsl)], writes=[(b_sg, gsl)])
                        op("V", lambda e, mb=mb, gsl=gsl: e.scalar_tensor_tensor(
                            h_buf[:, ts(mb, 512)], p_sc1[:, ts(gsl, 512)], 0.0,
                            sg[:, ts(gsl, 512)], ALU.add, ALU.mult),
                           reads=[(b_psc[1], gsl), (b_sg, gsl)],
                           writes=[(b_h, mb)])

                PH = [p_ot[:, 0:1024], p_ot[:, 1024:2048],
                      p_sc0[:, 0:1024], p_sc1[:, 0:1024]]
                PHB = [[(b_pot, 0)], [(b_pot, 1)], WHOLE0, WHOLE1]
                for t_i, (kc0, kcn) in enumerate(w2thirds):
                    w2_load(e_i, t_i)
                    slot = (e_i * 3 + t_i) % 2
                    for kcl in range(kcn):
                        kc = kc0 + kcl
                        for t4 in range(4):
                            for n2 in range(2):
                                op("T", lambda e, kc=kc, t4=t4, n2=n2, slot=slot,
                                   kcl=kcl:
                                   e.matmul(
                                       PH[t4][:, ts(n2, 512)],
                                       h_buf[:, kc * 512 + t4 * 128:
                                             kc * 512 + t4 * 128 + 128],
                                       w2s[:, slot * 8 * D + kcl * D + n2 * 512:
                                           slot * 8 * D + kcl * D + n2 * 512
                                           + 512],
                                       start=(kc == 0), stop=(kc == 21)),
                                   reads=[(b_h, kc), (b_w2, slot)],
                                   writes=PHB[t4])
                for t4 in range(4):
                    b_i = t4 // 2
                    tok0 = e_i * 256 + (t4 % 2) * 128
                    sl = t4 % 2
                    xt = x_tile[:, ts(sl, D)]
                    dma("Y", xt, xsrc[b_i, tok0:tok0 + 128, :],
                        reads=[xdram_buf(xsrc, b_i)], writes=[(b_xt, sl)])
                    ot = o_tile[:, ts(sl, D)]
                    op("V", lambda e, ot=ot, t4=t4, xt=xt: e.tensor_add(
                        ot, PH[t4][:, 0:1024], xt),
                       reads=PHB[t4] + [(b_xt, sl)], writes=[(b_ot_t, sl)])
                    dma("Y", xdst[b_i, tok0:tok0 + 128, :], ot,
                        reads=[(b_ot_t, sl)], writes=[xdram_buf(xdst, b_i)],
                        force_inc=(xdst is x_out))

        def run_phase(gen, retire_tokens):
            newbufs = next(gen)
            Sched.fence_into(newbufs, retire_tokens)
            try:
                next(gen)
            except StopIteration:
                pass
            return Sched.retire(newbufs)

        chain = [x_in, xA, xB, xA]
        outs = [xA, xB, xA, x_out]
        outs[n_sub - 1] = x_out

        retire_tokens = Sched.retire(phase0_bufs)
        for si in range(n_sub):
            if si in (0, 2):
                g = emit_attention("a" if si == 0 else "b", chain[si], outs[si])
            else:
                g = emit_ffn(chain[si], outs[si], 0 if si == 1 else 1)
            retire_tokens = run_phase(g, retire_tokens)

        bo = [xdram_buf(x_out, b) for b in range(NB)]
        op("Y", lambda e: e.drain(), reads=bo, force_inc=True)

        S_.emit(es)
    return nc



_CACHE = {}


def _get_program(n_sub=4):
    if n_sub not in _CACHE:
        _CACHE[n_sub] = build_program(n_sub)
    return _CACHE[n_sub]


def kernel(**inputs):
    nc = _get_program(4)
    sel = np.zeros((2, 128, 128), dtype=np.float32)
    sel[0, 0:64, :] = 1.0
    sel[1, 64:128, :] = 1.0
    irev = np.eye(128, dtype=np.float32)[::-1].copy()
    common = {
        "aWq": inputs["L0_attn_a_Wq"], "aWk": inputs["L0_attn_a_Wk"],
        "aWv": inputs["L0_attn_a_Wv"], "aWo": inputs["L0_attn_a_Wo"],
        "abq": inputs["L0_attn_a_bq"].reshape(8, 128).T,
        "abk": inputs["L0_attn_a_bk"].reshape(8, 128).T,
        "abv": inputs["L0_attn_a_bv"], "abo": inputs["L0_attn_a_bo"],
        "arel": inputs["L0_attn_a_rel_table"],
        "bscal": inputs["L0_attn_b_scaling"],
        "bqkvw": inputs["L0_attn_b_qkv_w"], "bqkvb": inputs["L0_attn_b_qkv_b"],
        "bqkvbT": inputs["L0_attn_b_qkv_b"].reshape(12, 128).T,
        "bow": inputs["L0_attn_b_o_w"], "bob": inputs["L0_attn_b_o_b"],
        "fw1": inputs["L0_ffn_w1"], "fw3": inputs["L0_ffn_w3"],
        "fw2": inputs["L0_ffn_w2"],
        "c_irev": irev, "c_sel": sel,
    }
    common = {k: np.ascontiguousarray(v, dtype=np.float32)
              for k, v in common.items()}
    x = np.ascontiguousarray(inputs["L0_attn_a_x"], dtype=np.float32)
    in_maps = []
    for c in range(8):
        m = dict(common)
        m["x"] = x[c * NB:(c + 1) * NB]
        in_maps.append(m)
    res = run_bass_kernel_spmd(nc, in_maps, core_ids=list(range(8)))
    out = np.concatenate([r["out"] for r in res.results], axis=0)
    return out


# version 16
# speedup vs baseline: 1.1782x; 1.1782x over previous
import numpy as np
from contextlib import ExitStack

import concourse.bass as bass
import concourse.mybir as mybir
from concourse.ap import AP
from concourse.bass_utils import run_bass_kernel_spmd

F32 = mybir.dt.float32
BF16 = mybir.dt.bfloat16
AF = mybir.ActivationFunctionType
ALU = mybir.AluOpType
AXX = mybir.AxisListType.X

D = 1024
S = 2048
NB = 2
NH = 16
DK = 64
DFF = 2816
NE = 8
KB = S // 128
TB = S // 128
EPS = 1e-5
SCALE_A = 1.0 / float(np.sqrt(DK))
R_SOFTPLUS_0 = 1.442695041

STEP = {"exec": 1, "dma": 16}


def ts(i, n):
    return slice(i * n, (i + 1) * n)



class Buf:
    __slots__ = ("w", "r", "name")

    def __init__(self, name, nslots=1):
        self.name = name
        self.w = [None] * nslots
        self.r = [[] for _ in range(nslots)]


class Stream:
    def __init__(self, name, section, kind):
        self.name = name
        self.section = section
        self.kind = kind
        self.ops = []
        self.sem = None
        self.rank = None


class Sched:
    def __init__(self, nc):
        self.nc = nc
        self.streams = {}
        for name, section, kind in (
            ("T", "tensor", "exec"),
            ("V", "vector", "exec"),
            ("S", "scalar", "exec"),
            ("Y", "sync", "dma"),
            ("G", "gpsimd", "dma"),
        ):
            self.streams[name] = Stream(name, section, kind)

    def op(self, sname, fn, reads=(), writes=(), force_inc=False):
        st = self.streams[sname]
        seq = len(st.ops)
        waits = {}

        def add_wait(tok):
            if tok is None:
                return
            tsid, tseq = tok
            if tsid == sname and sname == "T":
                return
            if self.streams[tsid].kind == "dma":
                key = (tsid, tseq % Sched.DMA_POOL)
            else:
                key = tsid
            if waits.get(key, -1) < tseq:
                waits[key] = tseq

        def norm(e):
            return e if isinstance(e, tuple) else (e, 0)

        for e in reads:
            b, sl = norm(e)
            add_wait(b.w[sl])
        for e in writes:
            b, sl = norm(e)
            add_wait(b.w[sl])
            for t in b.r[sl]:
                add_wait(t)
        st.ops.append([fn, waits, force_inc or st.kind == "dma"])
        tok = (sname, seq)
        for e in reads:
            b, sl = norm(e)
            b.r[sl].append(tok)
        for e in writes:
            b, sl = norm(e)
            b.w[sl] = tok
            b.r[sl] = []
        for key, tseq in waits.items():
            tsid = key[0] if isinstance(key, tuple) else key
            self.streams[tsid].ops[tseq][2] = True
        return tok

    @staticmethod
    def retire(bufs):
        toks = []
        for b in bufs:
            for sl in range(len(b.w)):
                if b.w[sl] is not None:
                    toks.append(b.w[sl])
                toks.extend(b.r[sl])
        return toks

    @staticmethod
    def fence_into(newbufs, tokens):
        for b in newbufs:
            for sl in range(len(b.w)):
                b.r[sl] = list(tokens) + b.r[sl]

    DMA_POOL = 16

    def emit(self, es):
        K = Sched.DMA_POOL
        for st in self.streams.values():
            if st.kind == "dma":
                st.sem = [es.enter_context(self.nc.semaphore(f"sem_{st.name}{i}"))
                          for i in range(K)]
            else:
                st.sem = es.enter_context(self.nc.semaphore("sem_" + st.name))
                st.rank = []
                c = 0
                for rec in st.ops:
                    if rec[2]:
                        c += 1
                    st.rank.append(c)
        blk = es.enter_context(self.nc.Block())
        sched = self

        def wait_for(eng, hw, tsid, tseq):
            tst = sched.streams[tsid]
            if tst.kind == "dma":
                si = tseq % K
                val = (tseq // K + 1) * 16
                key = (tsid, si)
                sem = tst.sem[si]
            else:
                val = tst.rank[tseq] * 1
                key = tsid
                sem = tst.sem
            if hw.get(key, -1) >= val:
                return
            hw[key] = val
            eng.wait_ge(sem, val)

        def make_section(sn):
            st = sched.streams[sn]

            def body(eng):
                hw = {}
                for seq, rec in enumerate(st.ops):
                    fn, waits, inc = rec
                    if st.kind == "dma" and seq >= K:
                        wait_for(eng, hw, sn, seq - K)
                    for key in sorted(waits, key=str):
                        tsid = key[0] if isinstance(key, tuple) else key
                        wait_for(eng, hw, tsid, waits[key])
                    ins = fn(eng)
                    if st.kind == "dma":
                        ins.then_inc(st.sem[seq % K], 16)
                    elif inc:
                        ins.then_inc(st.sem, 1)

            return body

        blk.tensor(make_section("T"))
        blk.vector(make_section("V"))
        blk.scalar(make_section("S"))
        blk.sync(make_section("Y"))
        blk.gpsimd(make_section("G"))



class Arena:

    def __init__(self, nc, base_addr, size):
        self.nc = nc
        self.base = base_addr
        self.size = size
        self.off = 0
        self.n = 0

    def place(self, name, shape, dtype, at=None):
        per_part = int(np.prod(shape[1:])) * mybir.dt.size(dtype)
        if at is None:
            at = (self.off + 31) & ~31
            self.off = at + per_part
        assert at + per_part <= self.size, (
            f"arena overflow: {name} at {at}+{per_part} > {self.size}")
        self.n += 1
        h = self.nc.virtual_sbuf_tensor(f"{name}_{self.n}", list(shape), dtype)
        self.nc._tensor(h.name, list(h.shape), h.dtype, type="SB")
        m = self.nc.lookup_mloc(h)
        m.allocated = True
        m.addr = self.base + at
        return h, at



def build_program(n_sub=4):
    nc = bass.Bass("TRN2", target_bir_lowering=False)

    x_in = nc.dram_tensor("x", [NB, S, D], F32, kind="ExternalInput")
    aWq = nc.dram_tensor("aWq", [D, D], F32, kind="ExternalInput")
    aWk = nc.dram_tensor("aWk", [D, D], F32, kind="ExternalInput")
    aWv = nc.dram_tensor("aWv", [D, D], F32, kind="ExternalInput")
    aWo = nc.dram_tensor("aWo", [D, D], F32, kind="ExternalInput")
    abq = nc.dram_tensor("abq", [128, 8], F32, kind="ExternalInput")
    abk = nc.dram_tensor("abk", [128, 8], F32, kind="ExternalInput")
    abv = nc.dram_tensor("abv", [D], F32, kind="ExternalInput")
    abo = nc.dram_tensor("abo", [D], F32, kind="ExternalInput")
    arel = nc.dram_tensor("arel", [1023, 16], F32, kind="ExternalInput")
    bscal = nc.dram_tensor("bscal", [64], F32, kind="ExternalInput")
    bqkvw = nc.dram_tensor("bqkvw", [1536, D], F32, kind="ExternalInput")
    bqkvb = nc.dram_tensor("bqkvb", [1536], F32, kind="ExternalInput")
    bqkvbT = nc.dram_tensor("bqkvbT", [128, 12], F32, kind="ExternalInput")
    bow = nc.dram_tensor("bow", [D, D], F32, kind="ExternalInput")
    bob = nc.dram_tensor("bob", [D], F32, kind="ExternalInput")
    fw1 = nc.dram_tensor("fw1", [NE, D, DFF], F32, kind="ExternalInput")
    fw3 = nc.dram_tensor("fw3", [NE, D, DFF], F32, kind="ExternalInput")
    fw2 = nc.dram_tensor("fw2", [NE, DFF, D], F32, kind="ExternalInput")
    c_irev = nc.dram_tensor("c_irev", [128, 128], F32, kind="ExternalInput")
    c_sel = nc.dram_tensor("c_sel", [2, 128, 128], F32, kind="ExternalInput")

    x_out = nc.dram_tensor("out", [NB, S, D], F32, kind="ExternalOutput")

    xA = nc.dram_tensor("xA", [NB, S, D], F32)
    xB = nc.dram_tensor("xB", [NB, S, D], F32)
    lnx_dram = nc.dram_tensor("lnx_dram", [NB, S, D], BF16)
    ext_dram = nc.dram_tensor("ext_dram", [NH, 4096], BF16)
    den_dram = nc.dram_tensor("den_dram", [2, S], F32)
    rec_dram = nc.dram_tensor("rec_dram", [2, S], F32)
    ot_dram = nc.dram_tensor("ot_dram", [2, 64, S], BF16)
    wTb_dram = nc.dram_tensor("wTb_dram", [1536, D], BF16)
    kv_dram = nc.dram_tensor("kv_dram", [128, 2 * S], BF16)
    wTo_dram = nc.dram_tensor("wTo_dram", [D, D], BF16)
    w2bf_dram = nc.dram_tensor("w2bf_dram", [NE, DFF, D], BF16)
    w1bf_dram = nc.dram_tensor("w1bf_dram", [NE, D, DFF], BF16)
    w3bf_dram = nc.dram_tensor("w3bf_dram", [NE, D, DFF], BF16)

    es = ExitStack()
    with es:
        S_ = Sched(nc)
        BUF = {}

        def buf(name, nslots=1):
            if name not in BUF:
                BUF[name] = Buf(name, nslots)
            return BUF[name]

        op = S_.op

        def dma(q, out, in_, reads=(), writes=(), force_inc=False):
            return op(q, lambda e, out=out, in_=in_: e.dma_start(out=out, in_=in_),
                      reads=reads, writes=writes, force_inc=force_inc)

        def dmaT(q, out, in_, reads=(), writes=()):
            return op(q, lambda e, out=out, in_=in_:
                      e.dma_start_transpose(out=out, in_=in_),
                      reads=reads, writes=writes)

        ARENA = 178688
        arena_h = nc.alloc_sbuf_tensor("arena", [128, ARENA], mybir.dt.uint8)
        arena_base = nc.lookup_mloc(arena_h).addr

        sb_p = lambda n, sh, dt: es.enter_context(nc.sbuf_tensor(n, sh, dt))
        irev = sb_p("irev", [128, 128], BF16)
        selb = sb_p("selb", [128, 256], BF16)
        biases = sb_p("biases", [128, 28], F32)
        brow = sb_p("brow", [1, 3 * D + 256], BF16)
        scol = sb_p("scol", [128, 8], F32)
        bnd = sb_p("bnd", [128, 36], F32)
        mv = sb_p("mv", [128, 4], F32)
        stats = sb_p("stats", [128, 16], F32)
        rsq = sb_p("rsq", [128, 64], F32)
        epsc = sb_p("epsc", [128, 1], F32)
        w2cv = sb_p("w2cv", [128, 2048], BF16)
        x_tile = sb_p("x_tile", [128, 2 * D], F32)
        o_tile = sb_p("o_tile", [128, 2 * D], F32)
        lnx_tile = sb_p("lnx_tile", [128, 2 * D], BF16)

        p_sc0 = es.enter_context(nc.psum_tensor("p_sc0", [128, 1024], F32))
        p_sc1 = es.enter_context(nc.psum_tensor("p_sc1", [128, 1024], F32))
        p_ot = es.enter_context(nc.psum_tensor("p_ot", [128, 2048], F32))
        b_psc = [buf("p_sc0", 2), buf("p_sc1", 2)]
        b_pot = buf("p_ot", 2)
        PSC = (p_sc0, p_sc1)
        WHOLE0 = [(b_psc[0], 0), (b_psc[0], 1)]
        WHOLE1 = [(b_psc[1], 0), (b_psc[1], 1)]
        WHOLEO = [(b_pot, 0), (b_pot, 1)]

        b_const = buf("const")
        b_small = buf("small")
        b_ext = buf("ext_dram")
        b_lnd = [buf(("lnxd", b)) for b in range(NB)]
        b_w2cv = buf("w2cv")
        b_lnxm = buf("lnx_tile", 2)
        b_xt = buf("x_tile", 2)
        b_ot_t = buf("o_tile", 2)
        b_stats = buf("stats_mv")
        b_bnd = buf("bnd")
        b_rsq = buf("rsq", 2)
        b_dend = buf("den_dram", 2)
        b_recd = buf("rec_dram", 2)
        b_otd = buf("ot_dram", 2)
        b_w2bf = [buf(("w2bf", e)) for e in range(NE)]
        b_w13bf = [[buf(("w13bf", m, e)) for e in range(NE)] for m in range(2)]

        def xdram_buf(t, b):
            return buf(("xd", t.name, b))

        ph0 = Arena(nc, arena_base, ARENA)
        flat_rel, _ = ph0.place("flat_rel", [1, 16368], F32)
        ext_row, _ = ph0.place("ext_row", [1, 4096], F32)
        ext_bfr, _ = ph0.place("ext_bfr", [1, 4096], BF16)
        b_flat = buf("flat_rel")
        b_extr = buf("ext_row")
        b_extb = buf("ext_bfr")

        dma("G", irev[:, :], c_irev[:, :], writes=[b_const])
        op("V", lambda e: e.memset(epsc[:, :], EPS), writes=[b_small])
        dma("G", selb[:, :].rearrange("p (s m) -> p s m", s=2),
            c_sel[:, :, :].rearrange("s p m -> p s m"), writes=[b_const])
        dma("Y", biases[:, 0:8], abq[:, :], writes=[b_small])
        dma("Y", biases[:, 8:16], abk[:, :], writes=[b_small])
        dma("Y", biases[:, 16:28], bqkvbT[:, :], writes=[b_small])
        dma("G", brow[0:1, 0:D], abo[:].unsqueeze(0), writes=[b_small])
        dma("G", brow[0:1, D:2 * D], bob[:].unsqueeze(0), writes=[b_small])
        dma("G", brow[0:1, 2 * D:3 * D], abv[:].unsqueeze(0),
            writes=[b_small])
        dma("G", brow[0:1, 3 * D:3 * D + 256],
            bqkvb[1280:1536].unsqueeze(0), writes=[b_small])
        dma("Y", scol[:, 0:1], AP(bscal, 0, [[0, 2], [1, 64]]), writes=[b_small])
        op("S", lambda e: e.activation(scol[:, 1:2], scol[:, 0:1], AF.Abs),
           reads=[b_small], writes=[b_small])
        op("S", lambda e: e.activation(scol[:, 2:3], scol[:, 1:2], AF.Exp,
                                       scale=-1.0),
           reads=[b_small], writes=[b_small])
        op("S", lambda e: e.activation(scol[:, 2:3], scol[:, 2:3], AF.Ln, bias=1.0),
           reads=[b_small], writes=[b_small])
        op("V", lambda e: e.tensor_scalar_max(scol[:, 3:4], scol[:, 0:1], 0.0),
           reads=[b_small], writes=[b_small])
        op("V", lambda e: e.tensor_add(scol[:, 3:4], scol[:, 3:4], scol[:, 2:3]),
           reads=[b_small], writes=[b_small])
        op("V", lambda e: e.tensor_scalar_mul(
            scol[:, 3:4], scol[:, 3:4], float(R_SOFTPLUS_0 / np.sqrt(DK))),
           reads=[b_small], writes=[b_small])

        dma("Y", flat_rel[0:1, :], arel[:, :].rearrange("a b -> (a b)").unsqueeze(0),
            writes=[b_flat])
        for h in range(NH):
            op("V", lambda e, h=h: e.tensor_copy(
                ext_row[0:1, 0:1023],
                AP(flat_rel.tensor if hasattr(flat_rel, "tensor") else flat_rel,
                   h, [[16368, 1], [16, 1023]])
                if False else
                AP(flat_rel, h, [[16368, 1], [16, 1023]])),
               reads=[b_flat], writes=[b_extr])
            op("V", lambda e, h=h: e.tensor_copy(
                ext_row[0:1, 1023:4096],
                AP(flat_rel, 1022 * 16 + h, [[16368, 1], [0, 3073]])),
               reads=[b_flat], writes=[b_extr])
            op("S", lambda e: e.activation(ext_bfr[0:1, :], ext_row[0:1, :],
                                           AF.Copy, scale=float(1.0 / SCALE_A)),
               reads=[b_extr], writes=[b_extb])
            dma("Y", ext_dram[h, :].unsqueeze(0), ext_bfr[0:1, :],
                reads=[b_extb], writes=[b_ext])

        for r0 in range(0, 1536, 256):
            dma("G", w2cv[:, 0:2048].rearrange("p (c m) -> p c m", m=D),
                bqkvw[r0:r0 + 256, :].rearrange("(c p) m -> p c m", p=128),
                writes=[b_w2cv])
            dma("G", wTb_dram[r0:r0 + 256, :].rearrange("(c p) m -> p c m", p=128),
                w2cv[:, 0:2048].rearrange("p (c m) -> p c m", m=D),
                reads=[b_w2cv], writes=[buf("wTb_dram")])
        for r0 in range(0, D, 256):
            dma("G", w2cv[:, 0:2048].rearrange("p (c m) -> p c m", m=D),
                bow[r0:r0 + 256, :].rearrange("(c p) m -> p c m", p=128),
                writes=[b_w2cv])
            dma("G", wTo_dram[r0:r0 + 256, :].rearrange("(c p) m -> p c m", p=128),
                w2cv[:, 0:2048].rearrange("p (c m) -> p c m", m=D),
                reads=[b_w2cv], writes=[buf("wTo_dram")])

        phase0_bufs = [b_flat, b_extr, b_extb]

        def emit_ln(xsrc, b):
            for tb in range(TB):
                sl = tb % 2
                xt = x_tile[:, ts(sl, D)]
                dma("Y", xt, xsrc[b, ts(tb, 128), :],
                    reads=[xdram_buf(xsrc, b)], writes=[(b_xt, sl)])
                op("V", lambda e, xt=xt: e.bn_stats(stats[:, 0:6], xt[:, 0:512]),
                   reads=[(b_xt, sl)], writes=[b_stats])
                op("V", lambda e, xt=xt: e.bn_stats(stats[:, 6:12], xt[:, 512:1024]),
                   reads=[(b_xt, sl)], writes=[b_stats])
                op("V", lambda e: e.bn_aggr(mv[:, 0:2], stats[:, 0:12]),
                   reads=[b_stats], writes=[b_stats])
                op("S", lambda e: e.activation(mv[:, 2:3], mv[:, 1:2], AF.Ln,
                                               bias=epsc[:, 0:1]),
                   reads=[b_stats], writes=[b_stats])
                op("S", lambda e: e.activation(mv[:, 2:3], mv[:, 2:3], AF.Exp,
                                               scale=-0.5),
                   reads=[b_stats], writes=[b_stats])
                op("V", lambda e, xt=xt, sl=sl: e.tensor_scalar(
                    lnx_tile[:, ts(sl, D)], xt, mv[:, 0:1], mv[:, 2:3],
                    ALU.subtract, ALU.mult),
                   reads=[(b_xt, sl), b_stats], writes=[(b_lnxm, sl)])
                dma("Y", lnx_dram[b, ts(tb, 128), :], lnx_tile[:, ts(sl, D)],
                    reads=[(b_lnxm, sl)], writes=[b_lnd[b]])

        def emit_lnxT(dst, bdst, b):
            for c in range(8):
                dmaT("Y", dst[:, ts(c, S)], lnx_dram[b, :, ts(c, 128)],
                     reads=[b_lnd[b]], writes=[bdst])

        def emit_oproj(xsrc, xdst, otbuf, b_otb, wmov, b_wmov, bias_row, b_i):
            for tb in range(TB):
                sl = tb % 2
                ph = p_ot[:, ts(sl, 1024)]
                xt = x_tile[:, ts(sl, D)]
                dma("Y", xt, xsrc[b_i, ts(tb, 128), :],
                    reads=[xdram_buf(xsrc, b_i)], writes=[(b_xt, sl)])
                for n2 in range(2):
                    op("T", lambda e, ph=ph, n2=n2, bias_row=bias_row:
                       e.matmul(ph[:, ts(n2, 512)], selb[0:1, 0:128],
                                bias_row[0:1, ts(n2, 512)], start=True, stop=False),
                       reads=[b_const, b_small], writes=[(b_pot, sl)])
                for c in range(8):
                    for n2 in range(2):
                        op("T", lambda e, ph=ph, c=c, n2=n2, tb=tb, otbuf=otbuf,
                           wmov=wmov:
                           e.matmul(ph[:, ts(n2, 512)],
                                    otbuf[:, c * S + tb * 128:c * S + tb * 128 + 128],
                                    wmov[:, c * D + n2 * 512:c * D + n2 * 512 + 512],
                                    start=False, stop=(c == 7)),
                           reads=[b_otb, b_wmov], writes=[(b_pot, sl)])
                ot = o_tile[:, ts(sl, D)]
                op("V", lambda e, ot=ot, ph=ph, xt=xt: e.tensor_add(ot, ph, xt),
                   reads=[(b_pot, sl), (b_xt, sl)], writes=[(b_ot_t, sl)])
                dma("Y", xdst[b_i, ts(tb, 128), :], ot,
                    reads=[(b_ot_t, sl)], writes=[xdram_buf(xdst, b_i)],
                    force_inc=(xdst is x_out))

        PSC3 = (p_sc0[:, 0:1024], p_sc1[:, 0:1024], p_ot[:, 0:1024])
        B3 = (WHOLE0, WHOLE1, [(b_pot, 0)])
        p_pd = p_ot[:, 0:1024]
        B_PD = [(b_pot, 0)]
        p_otr = p_ot[:, 1024:2048]
        B_OTR = [(b_pot, 1)]

        def emit_attention(kind, xsrc, xdst):
            is_a = kind == "a"
            vw = (NH if is_a else 4) * 65
            ar = Arena(nc, arena_base, ARENA)
            lnxT, _ = ar.place("lnxT" + kind, [128, 8 * S], BF16)
            otbuf, _ = ar.place("otbuf" + kind, [128, 8 * S], BF16)
            vtok, _ = ar.place("vtok" + kind, [128, KB * vw], BF16)
            qkT, _ = ar.place("qkT" + kind, [128, 4 * S], BF16)
            probs, pr_at = ar.place("probs" + kind, [128, 3 * 1024], BF16)
            sqt, _ = ar.place("sqt" + kind, [128, S], BF16, at=pr_at)
            otraw, _ = ar.place("otraw" + kind, [65, S], BF16)
            denrep, dr_at = ar.place("denrep" + kind, [65, 2 * 1024], F32)
            if is_a:
                wpair, _ = ar.place("wpair_a", [128, 2 * 2 * 1024], BF16)
                rep2e, re_at = ar.place("rep2e_a", [128, 4 * 3968], BF16)
                wmov, _ = ar.place("wmov_a", [128, 8 * D], BF16, at=re_at)
                wv, _ = ar.place("wv_a", [128, 8 * D], BF16,
                                 at=re_at + 16 * 1024)
                wqkvT = kvT = kvTalt = None
            else:
                wqkvT, _ = ar.place("wqkvT", [128, 8 * 1536], BF16)
                kvT, _ = ar.place("kvT", [128, 2 * S], BF16)
                kvTalt, _ = ar.place("kvTalt", [128, 2 * S], BF16)
                wmov, _ = ar.place("wmov_b", [128, 8 * D], BF16)
                wpair = rep2e = wv = None

            nb = []

            def mk(name, nslots=1):
                b = Buf((kind, name), nslots)
                nb.append(b)
                return b

            b_lnxT = mk("lnxT")
            b_otb = mk("otbuf")
            b_vt = mk("vtok")
            b_qk = mk("qkT", 2)
            b_pr = mk("probs", 3)
            b_sq = mk("sqt")
            b_otr = mk("otraw")
            b_dr = mk("denrep", 2)
            b_wp = mk("wpair", 2)
            b_rep = mk("rep2e", 2)
            b_wmov = mk("wmov")
            b_kvT = mk("kvT")
            b_kvTa = mk("kvTalt")
            b_bd = mk("bnd", 2)
            b_bkv = mk("bndkv")
            b_kvd = buf("kv_dram")
            if is_a:
                b_wmov = b_rep
            yield nb

            emit_ln(xsrc, 0)
            emit_ln(xsrc, 1)

            if not is_a:
                for c in range(8):
                    dmaT("Y", wqkvT[:, ts(c, 1536)], wTb_dram[:, ts(c, 128)],
                         reads=[buf("wTb_dram")], writes=[b_wp])
                for c in range(8):
                    dmaT("Y", wmov[:, ts(c, D)], wTo_dram[:, ts(c, 128)],
                         reads=[buf("wTo_dram")], writes=[b_wmov])

            def proj_chunk_pd(ncc, dst_ap, b_dst, lhs_fn, lhs_reads, bias_col,
                              scale_v):
                half = ncc % 2
                pd = p_pd[:, ts(half, 512)]
                for c in range(8):
                    op("T", lambda e, c=c, ncc=ncc, pd=pd, lhs_fn=lhs_fn:
                       e.matmul(pd, lhs_fn(c),
                                lnxT[:, c * S + ncc * 512:c * S + ncc * 512 + 512],
                                start=(c == 0), stop=(c == 7)),
                       reads=list(lhs_reads) + [b_lnxT], writes=B_PD)
                if scale_v is None:
                    op("V", lambda e, ncc=ncc, pd=pd: e.tensor_scalar(
                        dst_ap[:, ts(ncc, 512)], pd, bias_col, None, ALU.add),
                       reads=B_PD + [b_small], writes=[b_dst])
                else:
                    op("V", lambda e, ncc=ncc, pd=pd: e.tensor_scalar(
                        dst_ap[:, ts(ncc, 512)], pd, bias_col, scale_v,
                        ALU.add, ALU.mult),
                       reads=B_PD + [b_small], writes=[b_dst])

            def bound_head(src_ap, b_src, half, col, b_b):
                op("V", lambda e: e.tensor_mul(sqt[:, :], src_ap, src_ap),
                   reads=[b_src], writes=[b_sq, (b_pr, 0), (b_pr, 1)])
                base12 = (col // 12) * 12
                for i2 in range(2):
                    for j2 in range(2):
                        op("T", lambda e, i2=i2, j2=j2:
                           e.matmul(p_pd[:, ts(j2, 512)], selb[:, ts(half, 128)],
                                    sqt[:, ts(2 * i2 + j2, 512)],
                                    start=True, stop=True),
                           reads=[b_sq, (b_pr, 0), (b_pr, 1), b_const],
                           writes=B_PD)
                    op("V", lambda e, i2=i2, base12=base12: e.tensor_reduce(
                        bnd[:, base12 + 8 + i2:base12 + 9 + i2],
                        p_pd[:, 0:1024], AXX, ALU.max),
                       reads=B_PD, writes=[b_b])
                op("V", lambda e, col=col, base12=base12: e.tensor_max(
                    bnd[:, col:col + 1], bnd[:, base12 + 8:base12 + 9],
                    bnd[:, base12 + 9:base12 + 10]),
                   reads=[b_b], writes=[b_b])

            def finish_bound(qcol, kcol, outcol, scale, b_b, extra_reads=()):
                tcol = (outcol // 12) * 12 + 10
                op("V", lambda e: e.tensor_mul(
                    bnd[:, tcol:tcol + 1], bnd[:, qcol:qcol + 1],
                    bnd[:, kcol:kcol + 1]),
                   reads=[b_b] + list(extra_reads), writes=[b_b])
                op("S", lambda e: e.activation(bnd[:, tcol:tcol + 1],
                                               bnd[:, tcol:tcol + 1], AF.Ln),
                   reads=[b_b], writes=[b_b])
                op("S", lambda e: e.activation(bnd[:, tcol:tcol + 1],
                                               bnd[:, tcol:tcol + 1], AF.Exp,
                                               scale=0.5),
                   reads=[b_b], writes=[b_b])
                op("V", lambda e: e.tensor_scalar_mul(
                    bnd[:, outcol:outcol + 1], bnd[:, tcol:tcol + 1],
                    -float(scale)),
                   reads=[b_b], writes=[b_b])

            exp_scale = SCALE_A if is_a else 1.0

            pending = [None, None]

            def flush_pending(st_):
                if pending[st_] is not None:
                    pending[st_]()
                    pending[st_] = None

            def prep_steps(pair, b_i):
                py = pair % 2
                qk0 = 2 * S * py
                if is_a:
                    for mi, wm in enumerate((aWq, aWk)):
                        dma("G",
                            wpair[:, (py * 2 + mi) * D:(py * 2 + mi + 1) * D]
                            .rearrange("p (c m) -> p c m", m=128),
                            wm[:, ts(pair, 128)].rearrange("(c p) m -> p c m",
                                                           p=128),
                            writes=[(b_wp, py)])
                    for hh in range(2):
                        h = 2 * pair + hh
                        dma("G", rep2e[:, (py * 2 + hh) * 3968:
                                       (py * 2 + hh + 1) * 3968],
                            AP(ext_dram, h * 4096, [[1, 128], [1, 3968]]),
                            reads=[b_ext], writes=[(b_rep, py)])
                    yield
                    for ncc in range(4):
                        proj_chunk_pd(ncc, qkT[:, qk0:qk0 + S], (b_qk, py),
                                      lambda c, py=py:
                                      wpair[:, (py * 2) * D + c * 128:
                                            (py * 2) * D + c * 128 + 128],
                                      [(b_wp, py)], biases[:, pair:pair + 1],
                                      None)
                        yield
                    for ncc in range(4):
                        proj_chunk_pd(ncc, qkT[:, qk0 + S:qk0 + 2 * S],
                                      (b_qk, py),
                                      lambda c, py=py:
                                      wpair[:, (py * 2 + 1) * D + c * 128:
                                            (py * 2 + 1) * D + c * 128 + 128],
                                      [(b_wp, py)], biases[:, 8 + pair:9 + pair],
                                      None)
                        yield
                    for hh in range(2):
                        bound_head(qkT[:, qk0:qk0 + S], (b_qk, py), hh,
                                   py * 12 + 0 + hh, (b_bd, py))
                        yield
                        bound_head(qkT[:, qk0 + S:qk0 + 2 * S], (b_qk, py), hh,
                                   py * 12 + 2 + hh, (b_bd, py))
                        yield
                        finish_bound(py * 12 + 0 + hh, py * 12 + 2 + hh,
                                     py * 12 + 6 + hh, SCALE_A, (b_bd, py))
                        yield
                else:
                    for ncc in range(4):
                        proj_chunk_pd(ncc, qkT[:, qk0:qk0 + S], (b_qk, py),
                                      lambda c, pair=pair:
                                      wqkvT[:, c * 1536 + pair * 128:
                                            c * 1536 + pair * 128 + 128],
                                      [b_wp], biases[:, 16 + pair:17 + pair],
                                      scol[:, 3:4])
                        yield
                    for hh in range(2):
                        g = (2 * pair + hh) // 4
                        bound_head(qkT[:, qk0:qk0 + S], (b_qk, py), hh,
                                   py * 12 + 0 + hh, (b_bd, py))
                        yield
                        finish_bound(py * 12 + 0 + hh, 24 + g,
                                     py * 12 + 6 + hh, 1.0, (b_bd, py),
                                     extra_reads=[b_bkv])
                        yield
                op("V", lambda e: e.tensor_tensor(
                    bnd[:, py * 12 + 11:py * 12 + 12],
                    bnd[:, py * 12 + 6:py * 12 + 7],
                    bnd[:, py * 12 + 7:py * 12 + 8], ALU.min),
                   reads=[(b_bd, py)], writes=[(b_bd, py)])

            def sdpa_steps(pair, b_i):
                py = pair % 2
                qk0 = 2 * S * py
                h0 = 2 * pair
                HH = []
                for hh in range(2):
                    h = h0 + hh
                    if is_a:
                        HH.append((qkT[:, qk0 + S:qk0 + 2 * S], (b_qk, py),
                                   h * 65, (py * 2 + hh) * 3968))
                    else:
                        g = h // 4
                        if (g % 2) == (hh % 2):
                            HH.append((kvT[:, ts(g // 2, S)], b_kvT, g * 65,
                                       None))
                        else:
                            HH.append((kvTalt[:, ts(g // 2, S)], b_kvTa,
                                       g * 65, None))
                negb = bnd[:, py * 12 + 11:py * 12 + 12]

                for q4 in range(4):
                    half = q4 // 2
                    qoff = q4 * 512

                    def emit_sc(kb, q4=q4, qoff=qoff):
                        sl = kb % 3
                        pt = PSC3[sl]
                        pp = B3[sl]
                        for hh in range(2):
                            kT_ap, b_k, voff, repoff = HH[hh]
                            base = 64 * hh
                            op("T", lambda e, pt=pt, kb=kb, hh=hh, base=base,
                               kT_ap=kT_ap, b_k=b_k:
                               e.matmul(
                                   pt[:, ts(hh, 512)],
                                   kT_ap[base:base + 64, ts(kb, 128)],
                                   qkT[base:base + 64, qk0 + qoff:
                                       qk0 + qoff + 512],
                                   start=True, stop=not is_a),
                               reads=[b_k, (b_qk, py)], writes=pp)
                        if is_a:
                            for hh in range(2):
                                repoff = HH[hh][3]
                                off = repoff + 1920 - 128 * kb + qoff
                                op("T", lambda e, pt=pt, hh=hh, off=off:
                                   e.matmul(
                                       pt[:, ts(hh, 512)], irev[:, :],
                                       rep2e[:, off:off + 512],
                                       start=False, stop=True),
                                   reads=[(b_rep, py), b_const], writes=pp)

                    def emit_exp(kb, q4=q4, qoff=qoff):
                        sl = kb % 3
                        pt = PSC3[sl]
                        pp = B3[sl]
                        op("S", lambda e, pt=pt, sl=sl:
                           e.activation(probs[:, ts(sl, 1024)], pt[:, 0:1024],
                                        AF.Exp, bias=negb,
                                        scale=float(exp_scale)),
                           reads=pp + [(b_bd, py)], writes=[(b_pr, sl)])

                    def emit_pv(kb, q4=q4):
                        sl = kb % 3
                        for hh in range(2):
                            voff = HH[hh][2]
                            op("T", lambda e, kb=kb, sl=sl, hh=hh, voff=voff:
                               e.matmul(
                                   p_otr[0:65, ts(hh, 512)],
                                   vtok[:, kb * vw + voff:kb * vw + voff + 65],
                                   probs[:, sl * 1024 + hh * 512:
                                         sl * 1024 + hh * 512 + 512],
                                   start=(kb == 0), stop=(kb == KB - 1)),
                               reads=[(b_pr, sl), b_vt], writes=B_OTR)

                    emit_sc(0)
                    emit_sc(1)
                    for kb in range(KB):
                        emit_exp(kb)
                        if kb + 2 < KB:
                            emit_sc(kb + 2)
                        emit_pv(kb)
                        if kb % 2 == 1:
                            yield False

                    for hh in range(2):
                        even = hh == 0
                        if q4 % 2 == 0:
                            flush_pending(hh)
                        if even:
                            dst_q = otbuf[0:64, pair * S + qoff:
                                          pair * S + qoff + 512]
                            b_dst = b_otb
                        else:
                            dst_q = otraw[0:64, qoff:qoff + 512]
                            b_dst = b_otr
                        op("V", lambda e, dst_q=dst_q, hh=hh: e.tensor_copy(
                            dst_q, p_otr[0:64, ts(hh, 512)]),
                           reads=B_OTR, writes=[b_dst])
                        op("V", lambda e, hh=hh, q4=q4: e.tensor_copy(
                            denrep[64:65, hh * 1024 + (q4 % 2) * 512:
                                   hh * 1024 + (q4 % 2) * 512 + 512],
                            p_otr[64:65, ts(hh, 512)]),
                           reads=B_OTR, writes=[(b_dr, hh)])
                        if q4 % 2 == 1:
                            op_half = half
                            dma("Y", den_dram[hh, ts(op_half, 1024)].unsqueeze(0),
                                denrep[64:65, ts(hh, 1024)],
                                reads=[(b_dr, hh)], writes=[(b_dend, hh)])
                            dma("Y", rsq[:, hh * 32 + op_half * 8:
                                         hh * 32 + op_half * 8 + 8],
                                den_dram[hh, ts(op_half, 1024)]
                                .rearrange("(p c) -> p c", c=8),
                                reads=[(b_dend, hh)], writes=[(b_rsq, hh)])
                            op("V", lambda e, hh=hh, op_half=op_half:
                               e.reciprocal(
                                   rsq[:, hh * 32 + 16 + op_half * 8:
                                       hh * 32 + 16 + op_half * 8 + 8],
                                   rsq[:, hh * 32 + op_half * 8:
                                       hh * 32 + op_half * 8 + 8]),
                               reads=[(b_rsq, hh)], writes=[(b_rsq, hh)])
                            dma("Y", rec_dram[hh, ts(op_half, 1024)]
                                .rearrange("(p c) -> p c", c=8),
                                rsq[:, hh * 32 + 16 + op_half * 8:
                                    hh * 32 + 16 + op_half * 8 + 8],
                                reads=[(b_rsq, hh)], writes=[(b_recd, hh)])
                            dma("Y", denrep[0:64, ts(hh, 1024)],
                                AP(rec_dram, hh * S + op_half * 1024,
                                   [[0, 64], [1, 1024]]),
                                reads=[(b_recd, hh)], writes=[(b_dr, hh)])

                            def fin(hh=hh, op_half=op_half, even=even,
                                    pair=pair):
                                if even:
                                    dst_h = otbuf[0:64,
                                                  pair * S + op_half * 1024:
                                                  pair * S + op_half * 1024
                                                  + 1024]
                                    b_dsth = b_otb
                                else:
                                    dst_h = otraw[0:64, ts(op_half, 1024)]
                                    b_dsth = b_otr
                                op("V", lambda e: e.tensor_mul(
                                    dst_h, dst_h, denrep[0:64, ts(hh, 1024)]),
                                   reads=[(b_dr, hh), b_dsth], writes=[b_dsth])
                                if not even:
                                    dma("G", ot_dram[hh, :, ts(op_half, 1024)],
                                        otraw[0:64, ts(op_half, 1024)],
                                        reads=[b_dsth], writes=[(b_otd, hh)])
                                    dma("G", otbuf[64:128,
                                                   pair * S + op_half * 1024:
                                                   pair * S + op_half * 1024
                                                   + 1024],
                                        ot_dram[hh, :, ts(op_half, 1024)],
                                        reads=[(b_otd, hh)], writes=[b_otb])

                            pending[hh] = fin
                    yield True

            ones_done = [False]
            for b_i in range(NB):
                emit_lnxT(lnxT, b_lnxT, b_i)

                if is_a:
                    dma("G", wv[:, :].rearrange("p (c m) -> p c m", m=D),
                        aWv[:, :].rearrange("(c p) m -> p c m", p=128),
                        writes=[(b_rep, 0), (b_rep, 1)])
                    if not ones_done[0]:
                        op("V", lambda e: e.memset(
                            vtok[:, :].rearrange("p (t h o) -> p t h o",
                                                 h=NH, o=65)
                            [:, :, :, 64:65], 1.0), writes=[b_vt])
                        ones_done[0] = True
                    for tb in range(TB):
                        pp = WHOLE0 if tb % 2 == 0 else WHOLE1
                        pt = PSC[tb % 2]
                        for n2 in range(2):
                            op("T", lambda e, pt=pt, n2=n2: e.matmul(
                                pt[:, ts(n2, 512)], selb[0:1, 0:128],
                                brow[0:1, 2 * D + n2 * 512:2 * D + n2 * 512 + 512],
                                start=True, stop=False),
                               reads=[b_const, b_small], writes=pp)
                            for c in range(8):
                                op("T", lambda e, tb=tb, c=c, n2=n2, pt=pt:
                                   e.matmul(
                                       pt[:, ts(n2, 512)],
                                       lnxT[:, c * S + tb * 128:
                                            c * S + tb * 128 + 128],
                                       wv[:, c * D + n2 * 512:
                                          c * D + n2 * 512 + 512],
                                       start=False, stop=(c == 7)),
                                   reads=[(b_rep, 0), (b_rep, 1), b_lnxT],
                                   writes=pp)
                        op("V", lambda e, tb=tb, pt=pt: e.tensor_copy(
                            vtok[:, tb * vw:(tb + 1) * vw]
                            .rearrange("p (h o) -> p h o", o=65)[:, :, 0:64],
                            pt[:, 0:1024].rearrange("p (h o) -> p h o", o=64)),
                           reads=pp, writes=[b_vt])
                else:
                    for blk2 in range(2):
                        for ncc in range(4):
                            proj_chunk_pd(ncc, kvT[:, ts(blk2, S)], b_kvT,
                                          lambda c, blk2=blk2:
                                          wqkvT[:, c * 1536 + 1024 + blk2 * 128:
                                                c * 1536 + 1024 + blk2 * 128
                                                + 128],
                                          [b_wp],
                                          biases[:, 24 + blk2:25 + blk2], None)
                    dma("Y", kv_dram[:, :], kvT[:, 0:2 * S],
                        reads=[b_kvT], writes=[b_kvd])
                    for blk2 in range(2):
                        dma("Y", kvTalt[0:64, ts(blk2, S)],
                            kv_dram[64:128, ts(blk2, S)],
                            reads=[b_kvd], writes=[b_kvTa])
                        dma("Y", kvTalt[64:128, ts(blk2, S)],
                            kv_dram[0:64, ts(blk2, S)],
                            reads=[b_kvd], writes=[b_kvTa])
                    if not ones_done[0]:
                        op("V", lambda e: e.memset(
                            vtok[:, :].rearrange("p (t h o) -> p t h o",
                                                 h=4, o=65)
                            [:, :, :, 64:65], 1.0), writes=[b_vt])
                        ones_done[0] = True
                    for tb in range(TB):
                        pp = WHOLE0 if tb % 2 == 0 else WHOLE1
                        pt = PSC[tb % 2]
                        op("T", lambda e, pt=pt: e.matmul(
                            pt[:, 0:256], selb[0:1, 0:128],
                            brow[0:1, 3 * D:3 * D + 256], start=True,
                            stop=False),
                           reads=[b_const, b_small], writes=pp)
                        for c in range(8):
                            op("T", lambda e, tb=tb, c=c, pt=pt: e.matmul(
                                pt[:, 0:256],
                                lnxT[:, c * S + tb * 128:c * S + tb * 128 + 128],
                                wqkvT[:, c * 1536 + 1280:c * 1536 + 1536],
                                start=False, stop=(c == 7)),
                               reads=[b_wp, b_lnxT], writes=pp)
                        for g in range(4):
                            op("V", lambda e, tb=tb, g=g, pt=pt: e.tensor_copy(
                                vtok[:, tb * vw + g * 65:tb * vw + g * 65 + 64],
                                pt[:, g * 64:g * 64 + 64]),
                               reads=pp, writes=[b_vt])
                    for g in range(4):
                        bound_head(kvT[:, ts(g // 2, S)], b_kvT, g % 2,
                                   24 + g, b_bkv)

                for _ in prep_steps(0, b_i):
                    pass
                for pair in range(8):
                    g_s = sdpa_steps(pair, b_i)
                    g_p = prep_steps(pair + 1, b_i) if pair < 7 else iter(())
                    for safe in g_s:
                        if safe:
                            for _ in range(4):
                                next(g_p, None)
                    for _ in g_p:
                        pass

                flush_pending(0)
                flush_pending(1)
                if is_a:
                    dma("G", wmov[:, :].rearrange("p (c m) -> p c m", m=D),
                        aWo[:, :].rearrange("(c p) m -> p c m", p=128),
                        writes=[(b_rep, 0), (b_rep, 1)])
                bias_row = brow[0:1, 0:D] if is_a else brow[0:1, D:2 * D]
                emit_oproj(xsrc, xdst, otbuf, b_otb, wmov, b_wmov, bias_row, b_i)

        def emit_ffn(xsrc, xdst, call_i):
            ar = Arena(nc, arena_base, ARENA)
            lnxTf, _ = ar.place("f_lnxT", [128, 8 * 2 * S], BF16)
            h_buf, _ = ar.place("f_h", [128, 22 * 512], BF16)
            w13, _ = ar.place("f_w13", [128, 4 * 8 * 768], BF16)
            w2s, _ = ar.place("f_w2", [128, 2 * 8 * 1024], BF16)
            sg, _ = ar.place("f_sg", [128, 2 * 512], BF16)

            nb = []

            def mk(name, nslots=1):
                b = Buf(("f", name), nslots)
                nb.append(b)
                return b

            b_lnxT = mk("lnxTf", NE)
            b_h = mk("h", 22)
            b_w13 = mk("w13", 4)
            b_w2 = mk("w2", 2)
            b_sg = mk("sg", 2)
            yield nb

            emit_ln(xsrc, 0)
            emit_ln(xsrc, 1)
            for e_i in range(NE):
                for b_i in range(NB):
                    for c in range(8):
                        dmaT("Y", lnxTf[:, c * 2 * S + e_i * 512 + b_i * 256:
                                        c * 2 * S + e_i * 512 + b_i * 256 + 256],
                             lnx_dram[b_i, ts(e_i, 256), ts(c, 128)],
                             reads=[b_lnd[b_i]], writes=[(b_lnxT, e_i)])

            quarters = [(0, 6), (6, 6), (12, 5), (17, 5)]
            w2thirds = [(0, 8), (8, 7), (15, 7)]

            def w13_load(e_i, q_i):
                mb0, mbn = quarters[q_i]
                for mat_i in range(2):
                    wd = (w1bf_dram, w3bf_dram)[mat_i]
                    slot = mat_i * 2 + (e_i * 4 + q_i) % 2
                    view = (w13[:, slot * 8 * 768:slot * 8 * 768 + 8 * mbn * 128]
                            .rearrange("p (c m) -> p c m", m=mbn * 128))
                    dview = (wd[e_i, :, mb0 * 128:(mb0 + mbn) * 128]
                             .rearrange("(c p) m -> p c m", p=128))
                    if call_i == 0:
                        wm = (fw1, fw3)[mat_i]
                        dma("G", view,
                            wm[e_i, :, mb0 * 128:(mb0 + mbn) * 128]
                            .rearrange("(c p) m -> p c m", p=128),
                            writes=[(b_w13, slot)])
                        dma("Y", dview, view,
                            reads=[(b_w13, slot)], writes=[b_w13bf[mat_i][e_i]])
                    else:
                        dma("Y", view, dview,
                            reads=[b_w13bf[mat_i][e_i]], writes=[(b_w13, slot)])

            def w2_load(e_i, t_i):
                kc0, kcn = w2thirds[t_i]
                slot = (e_i * 3 + t_i) % 2
                view = (w2s[:, slot * 8 * D:slot * 8 * D + kcn * D]
                        .rearrange("p (c m) -> p c m", m=D))
                dview = (w2bf_dram[e_i, slice(kc0 * 128, (kc0 + kcn) * 128), :]
                         .rearrange("(c p) m -> p c m", p=128))
                if call_i == 0:
                    dma("G", view,
                        fw2[e_i, slice(kc0 * 128, (kc0 + kcn) * 128), :]
                        .rearrange("(c p) m -> p c m", p=128),
                        writes=[(b_w2, slot)])
                    dma("Y", dview, view,
                        reads=[(b_w2, slot)], writes=[b_w2bf[e_i]])
                else:
                    dma("Y", view, dview,
                        reads=[b_w2bf[e_i]], writes=[(b_w2, slot)])

            for e_i in range(NE):
                for q_i, (mb0, mbn) in enumerate(quarters):
                    w13_load(e_i, q_i)
                    slot1 = 0 * 2 + (e_i * 4 + q_i) % 2
                    slot3 = 1 * 2 + (e_i * 4 + q_i) % 2
                    for mbl in range(mbn):
                        mb = mb0 + mbl
                        gsl = mb % 2
                        for gi, slot in ((0, slot1), (1, slot3)):
                            pt = PSC[gi]
                            pp = (b_psc[gi], gsl)
                            for c in range(8):
                                op("T", lambda e, pt=pt, gsl=gsl, c=c,
                                   slot=slot, mbl=mbl, mbn=mbn, e_i=e_i:
                                   e.matmul(
                                       pt[:, ts(gsl, 512)],
                                       w13[:, slot * 8 * 768 + (c * mbn + mbl)
                                           * 128:
                                           slot * 8 * 768 + (c * mbn + mbl)
                                           * 128 + 128],
                                       lnxTf[:, c * 2 * S + e_i * 512:
                                             c * 2 * S + e_i * 512 + 512],
                                       start=(c == 0), stop=(c == 7)),
                                   reads=[(b_w13, slot), (b_lnxT, e_i)],
                                   writes=[pp])
                        op("S", lambda e, gsl=gsl: e.activation(
                            sg[:, ts(gsl, 512)], p_sc0[:, ts(gsl, 512)], AF.Silu),
                           reads=[(b_psc[0], gsl)], writes=[(b_sg, gsl)])
                        op("V", lambda e, mb=mb, gsl=gsl: e.scalar_tensor_tensor(
                            h_buf[:, ts(mb, 512)], p_sc1[:, ts(gsl, 512)], 0.0,
                            sg[:, ts(gsl, 512)], ALU.add, ALU.mult),
                           reads=[(b_psc[1], gsl), (b_sg, gsl)],
                           writes=[(b_h, mb)])

                PH = [p_ot[:, 0:1024], p_ot[:, 1024:2048],
                      p_sc0[:, 0:1024], p_sc1[:, 0:1024]]
                PHB = [[(b_pot, 0)], [(b_pot, 1)], WHOLE0, WHOLE1]
                for t_i, (kc0, kcn) in enumerate(w2thirds):
                    w2_load(e_i, t_i)
                    slot = (e_i * 3 + t_i) % 2
                    for kcl in range(kcn):
                        kc = kc0 + kcl
                        for t4 in range(4):
                            for n2 in range(2):
                                op("T", lambda e, kc=kc, t4=t4, n2=n2, slot=slot,
                                   kcl=kcl:
                                   e.matmul(
                                       PH[t4][:, ts(n2, 512)],
                                       h_buf[:, kc * 512 + t4 * 128:
                                             kc * 512 + t4 * 128 + 128],
                                       w2s[:, slot * 8 * D + kcl * D + n2 * 512:
                                           slot * 8 * D + kcl * D + n2 * 512
                                           + 512],
                                       start=(kc == 0), stop=(kc == 21)),
                                   reads=[(b_h, kc), (b_w2, slot)],
                                   writes=PHB[t4])
                for t4 in range(4):
                    b_i = t4 // 2
                    tok0 = e_i * 256 + (t4 % 2) * 128
                    sl = t4 % 2
                    xt = x_tile[:, ts(sl, D)]
                    dma("Y", xt, xsrc[b_i, tok0:tok0 + 128, :],
                        reads=[xdram_buf(xsrc, b_i)], writes=[(b_xt, sl)])
                    ot = o_tile[:, ts(sl, D)]
                    op("V", lambda e, ot=ot, t4=t4, xt=xt: e.tensor_add(
                        ot, PH[t4][:, 0:1024], xt),
                       reads=PHB[t4] + [(b_xt, sl)], writes=[(b_ot_t, sl)])
                    dma("Y", xdst[b_i, tok0:tok0 + 128, :], ot,
                        reads=[(b_ot_t, sl)], writes=[xdram_buf(xdst, b_i)],
                        force_inc=(xdst is x_out))

        def run_phase(gen, retire_tokens):
            newbufs = next(gen)
            Sched.fence_into(newbufs, retire_tokens)
            try:
                next(gen)
            except StopIteration:
                pass
            return Sched.retire(newbufs)

        chain = [x_in, xA, xB, xA]
        outs = [xA, xB, xA, x_out]
        outs[n_sub - 1] = x_out

        retire_tokens = Sched.retire(phase0_bufs)
        for si in range(n_sub):
            if si in (0, 2):
                g = emit_attention("a" if si == 0 else "b", chain[si], outs[si])
            else:
                g = emit_ffn(chain[si], outs[si], 0 if si == 1 else 1)
            retire_tokens = run_phase(g, retire_tokens)

        bo = [xdram_buf(x_out, b) for b in range(NB)]
        op("Y", lambda e: e.drain(), reads=bo, force_inc=True)

        S_.emit(es)
    return nc



_CACHE = {}


def _get_program(n_sub=4):
    if n_sub not in _CACHE:
        _CACHE[n_sub] = build_program(n_sub)
    return _CACHE[n_sub]


def kernel(**inputs):
    nc = _get_program(4)
    sel = np.zeros((2, 128, 128), dtype=np.float32)
    sel[0, 0:64, :] = 1.0
    sel[1, 64:128, :] = 1.0
    irev = np.eye(128, dtype=np.float32)[::-1].copy()
    common = {
        "aWq": inputs["L0_attn_a_Wq"], "aWk": inputs["L0_attn_a_Wk"],
        "aWv": inputs["L0_attn_a_Wv"], "aWo": inputs["L0_attn_a_Wo"],
        "abq": inputs["L0_attn_a_bq"].reshape(8, 128).T,
        "abk": inputs["L0_attn_a_bk"].reshape(8, 128).T,
        "abv": inputs["L0_attn_a_bv"], "abo": inputs["L0_attn_a_bo"],
        "arel": inputs["L0_attn_a_rel_table"],
        "bscal": inputs["L0_attn_b_scaling"],
        "bqkvw": inputs["L0_attn_b_qkv_w"], "bqkvb": inputs["L0_attn_b_qkv_b"],
        "bqkvbT": inputs["L0_attn_b_qkv_b"].reshape(12, 128).T,
        "bow": inputs["L0_attn_b_o_w"], "bob": inputs["L0_attn_b_o_b"],
        "fw1": inputs["L0_ffn_w1"], "fw3": inputs["L0_ffn_w3"],
        "fw2": inputs["L0_ffn_w2"],
        "c_irev": irev, "c_sel": sel,
    }
    common = {k: np.ascontiguousarray(v, dtype=np.float32)
              for k, v in common.items()}
    x = np.ascontiguousarray(inputs["L0_attn_a_x"], dtype=np.float32)
    in_maps = []
    for c in range(8):
        m = dict(common)
        m["x"] = x[c * NB:(c + 1) * NB]
        in_maps.append(m)
    res = run_bass_kernel_spmd(nc, in_maps, core_ids=list(range(8)))
    out = np.concatenate([r["out"] for r in res.results], axis=0)
    return out


# version 17
# speedup vs baseline: 1.3243x; 1.1240x over previous
import numpy as np
from contextlib import ExitStack

import concourse.bass as bass
import concourse.mybir as mybir
from concourse.ap import AP
from concourse.bass_utils import run_bass_kernel_spmd

F32 = mybir.dt.float32
BF16 = mybir.dt.bfloat16
AF = mybir.ActivationFunctionType
ALU = mybir.AluOpType
AXX = mybir.AxisListType.X

D = 1024
S = 2048
NB = 2
NH = 16
DK = 64
DFF = 2816
NE = 8
KB = S // 128
TB = S // 128
EPS = 1e-5
SCALE_A = 1.0 / float(np.sqrt(DK))
R_SOFTPLUS_0 = 1.442695041

STEP = {"exec": 1, "dma": 16}


def ts(i, n):
    return slice(i * n, (i + 1) * n)



class Buf:
    __slots__ = ("w", "r", "name")

    def __init__(self, name, nslots=1):
        self.name = name
        self.w = [None] * nslots
        self.r = [[] for _ in range(nslots)]


class Stream:
    def __init__(self, name, section, kind):
        self.name = name
        self.section = section
        self.kind = kind
        self.ops = []
        self.sem = None
        self.rank = None


class Sched:
    def __init__(self, nc):
        self.nc = nc
        self.streams = {}
        for name, section, kind in (
            ("T", "tensor", "exec"),
            ("V", "vector", "exec"),
            ("S", "scalar", "exec"),
            ("Y", "sync", "dma"),
            ("G", "gpsimd", "dma"),
        ):
            self.streams[name] = Stream(name, section, kind)

    def op(self, sname, fn, reads=(), writes=(), force_inc=False):
        st = self.streams[sname]
        seq = len(st.ops)
        waits = {}

        def add_wait(tok):
            if tok is None:
                return
            tsid, tseq = tok
            if tsid == sname and sname == "T":
                return
            if self.streams[tsid].kind == "dma":
                key = (tsid, tseq % Sched.DMA_POOL)
            else:
                key = tsid
            if waits.get(key, -1) < tseq:
                waits[key] = tseq

        def norm(e):
            return e if isinstance(e, tuple) else (e, 0)

        for e in reads:
            b, sl = norm(e)
            add_wait(b.w[sl])
        for e in writes:
            b, sl = norm(e)
            add_wait(b.w[sl])
            for t in b.r[sl]:
                add_wait(t)
        st.ops.append([fn, waits, force_inc or st.kind == "dma"])
        tok = (sname, seq)
        for e in reads:
            b, sl = norm(e)
            b.r[sl].append(tok)
        for e in writes:
            b, sl = norm(e)
            b.w[sl] = tok
            b.r[sl] = []
        for key, tseq in waits.items():
            tsid = key[0] if isinstance(key, tuple) else key
            self.streams[tsid].ops[tseq][2] = True
        return tok

    @staticmethod
    def retire(bufs):
        toks = []
        for b in bufs:
            for sl in range(len(b.w)):
                if b.w[sl] is not None:
                    toks.append(b.w[sl])
                toks.extend(b.r[sl])
        return toks

    @staticmethod
    def fence_into(newbufs, tokens):
        for b in newbufs:
            for sl in range(len(b.w)):
                b.r[sl] = list(tokens) + b.r[sl]

    DMA_POOL = 16

    def emit(self, es):
        K = Sched.DMA_POOL
        for st in self.streams.values():
            if st.kind == "dma":
                st.sem = [es.enter_context(self.nc.semaphore(f"sem_{st.name}{i}"))
                          for i in range(K)]
            else:
                st.sem = es.enter_context(self.nc.semaphore("sem_" + st.name))
                st.rank = []
                c = 0
                for rec in st.ops:
                    if rec[2]:
                        c += 1
                    st.rank.append(c)
        blk = es.enter_context(self.nc.Block())
        sched = self

        def wait_for(eng, hw, tsid, tseq):
            tst = sched.streams[tsid]
            if tst.kind == "dma":
                si = tseq % K
                val = (tseq // K + 1) * 16
                key = (tsid, si)
                sem = tst.sem[si]
            else:
                val = tst.rank[tseq] * 1
                key = tsid
                sem = tst.sem
            if hw.get(key, -1) >= val:
                return
            hw[key] = val
            eng.wait_ge(sem, val)

        def make_section(sn):
            st = sched.streams[sn]

            def body(eng):
                hw = {}
                for seq, rec in enumerate(st.ops):
                    fn, waits, inc = rec
                    if st.kind == "dma" and seq >= K:
                        wait_for(eng, hw, sn, seq - K)
                    for key in sorted(waits, key=str):
                        tsid = key[0] if isinstance(key, tuple) else key
                        wait_for(eng, hw, tsid, waits[key])
                    ins = fn(eng)
                    if st.kind == "dma":
                        ins.then_inc(st.sem[seq % K], 16)
                    elif inc:
                        ins.then_inc(st.sem, 1)

            return body

        blk.tensor(make_section("T"))
        blk.vector(make_section("V"))
        blk.scalar(make_section("S"))
        blk.sync(make_section("Y"))
        blk.gpsimd(make_section("G"))



class Arena:

    def __init__(self, nc, base_addr, size):
        self.nc = nc
        self.base = base_addr
        self.size = size
        self.off = 0
        self.n = 0

    def place(self, name, shape, dtype, at=None):
        per_part = int(np.prod(shape[1:])) * mybir.dt.size(dtype)
        if at is None:
            at = (self.off + 31) & ~31
            self.off = at + per_part
        assert at + per_part <= self.size, (
            f"arena overflow: {name} at {at}+{per_part} > {self.size}")
        self.n += 1
        h = self.nc.virtual_sbuf_tensor(f"{name}_{self.n}", list(shape), dtype)
        self.nc._tensor(h.name, list(h.shape), h.dtype, type="SB")
        m = self.nc.lookup_mloc(h)
        m.allocated = True
        m.addr = self.base + at
        return h, at



def build_program(n_sub=4):
    nc = bass.Bass("TRN2", target_bir_lowering=False)

    x_in = nc.dram_tensor("x", [NB, S, D], F32, kind="ExternalInput")
    aWq = nc.dram_tensor("aWq", [D, D], F32, kind="ExternalInput")
    aWk = nc.dram_tensor("aWk", [D, D], F32, kind="ExternalInput")
    aWv = nc.dram_tensor("aWv", [D, D], F32, kind="ExternalInput")
    aWo = nc.dram_tensor("aWo", [D, D], F32, kind="ExternalInput")
    abq = nc.dram_tensor("abq", [128, 8], F32, kind="ExternalInput")
    abk = nc.dram_tensor("abk", [128, 8], F32, kind="ExternalInput")
    abv = nc.dram_tensor("abv", [D], F32, kind="ExternalInput")
    abo = nc.dram_tensor("abo", [D], F32, kind="ExternalInput")
    arel = nc.dram_tensor("arel", [1023, 16], F32, kind="ExternalInput")
    bscal = nc.dram_tensor("bscal", [64], F32, kind="ExternalInput")
    bqkvw = nc.dram_tensor("bqkvw", [1536, D], F32, kind="ExternalInput")
    bqkvb = nc.dram_tensor("bqkvb", [1536], F32, kind="ExternalInput")
    bqkvbT = nc.dram_tensor("bqkvbT", [128, 12], F32, kind="ExternalInput")
    bow = nc.dram_tensor("bow", [D, D], F32, kind="ExternalInput")
    bob = nc.dram_tensor("bob", [D], F32, kind="ExternalInput")
    fw1 = nc.dram_tensor("fw1", [NE, D, DFF], F32, kind="ExternalInput")
    fw3 = nc.dram_tensor("fw3", [NE, D, DFF], F32, kind="ExternalInput")
    fw2 = nc.dram_tensor("fw2", [NE, DFF, D], F32, kind="ExternalInput")
    c_irev = nc.dram_tensor("c_irev", [128, 128], F32, kind="ExternalInput")
    c_sel = nc.dram_tensor("c_sel", [2, 128, 128], F32, kind="ExternalInput")

    x_out = nc.dram_tensor("out", [NB, S, D], F32, kind="ExternalOutput")

    xA = nc.dram_tensor("xA", [NB, S, D], F32)
    xB = nc.dram_tensor("xB", [NB, S, D], F32)
    lnx_dram = nc.dram_tensor("lnx_dram", [NB, S, D], BF16)
    ext_dram = nc.dram_tensor("ext_dram", [NH, 4096], BF16)
    den_dram = nc.dram_tensor("den_dram", [2, S], F32)
    rec_dram = nc.dram_tensor("rec_dram", [2, S], F32)
    ot_dram = nc.dram_tensor("ot_dram", [2, 64, S], BF16)
    wTb_dram = nc.dram_tensor("wTb_dram", [1536, D], BF16)
    kv_dram = nc.dram_tensor("kv_dram", [128, 2 * S], BF16)
    wTo_dram = nc.dram_tensor("wTo_dram", [D, D], BF16)
    w2bf_dram = nc.dram_tensor("w2bf_dram", [NE, DFF, D], BF16)
    w1bf_dram = nc.dram_tensor("w1bf_dram", [NE, D, DFF], BF16)
    w3bf_dram = nc.dram_tensor("w3bf_dram", [NE, D, DFF], BF16)

    es = ExitStack()
    with es:
        S_ = Sched(nc)
        BUF = {}

        def buf(name, nslots=1):
            if name not in BUF:
                BUF[name] = Buf(name, nslots)
            return BUF[name]

        op = S_.op

        def dma(q, out, in_, reads=(), writes=(), force_inc=False):
            return op(q, lambda e, out=out, in_=in_: e.dma_start(out=out, in_=in_),
                      reads=reads, writes=writes, force_inc=force_inc)

        def dmaT(q, out, in_, reads=(), writes=()):
            return op(q, lambda e, out=out, in_=in_:
                      e.dma_start_transpose(out=out, in_=in_),
                      reads=reads, writes=writes)

        ARENA = 178688
        arena_h = nc.alloc_sbuf_tensor("arena", [128, ARENA], mybir.dt.uint8)
        arena_base = nc.lookup_mloc(arena_h).addr

        sb_p = lambda n, sh, dt: es.enter_context(nc.sbuf_tensor(n, sh, dt))
        irev = sb_p("irev", [128, 128], BF16)
        selb = sb_p("selb", [128, 256], BF16)
        biases = sb_p("biases", [128, 28], F32)
        brow = sb_p("brow", [1, 3 * D + 256], BF16)
        scol = sb_p("scol", [128, 8], F32)
        bnd = sb_p("bnd", [128, 36], F32)
        mv = sb_p("mv", [128, 4], F32)
        stats = sb_p("stats", [128, 16], F32)
        rsq = sb_p("rsq", [128, 64], F32)
        epsc = sb_p("epsc", [128, 1], F32)
        w2cv = sb_p("w2cv", [128, 2048], BF16)
        x_tile = sb_p("x_tile", [128, 2 * D], F32)
        o_tile = sb_p("o_tile", [128, 2 * D], F32)
        lnx_tile = sb_p("lnx_tile", [128, 2 * D], BF16)

        p_sc0 = es.enter_context(nc.psum_tensor("p_sc0", [128, 1024], F32))
        p_sc1 = es.enter_context(nc.psum_tensor("p_sc1", [128, 1024], F32))
        p_ot = es.enter_context(nc.psum_tensor("p_ot", [128, 2048], F32))
        b_psc = [buf("p_sc0", 2), buf("p_sc1", 2)]
        b_pot = buf("p_ot", 2)
        PSC = (p_sc0, p_sc1)
        WHOLE0 = [(b_psc[0], 0), (b_psc[0], 1)]
        WHOLE1 = [(b_psc[1], 0), (b_psc[1], 1)]
        WHOLEO = [(b_pot, 0), (b_pot, 1)]

        b_const = buf("const")
        b_small = buf("small")
        b_ext = buf("ext_dram")
        b_lnd = [buf(("lnxd", b)) for b in range(NB)]
        b_w2cv = buf("w2cv")
        b_lnxm = buf("lnx_tile", 2)
        b_xt = buf("x_tile", 2)
        b_ot_t = buf("o_tile", 2)
        b_stats = buf("stats_mv")
        b_bnd = buf("bnd")
        b_rsq = buf("rsq", 2)
        b_dend = buf("den_dram", 2)
        b_recd = buf("rec_dram", 2)
        b_otd = buf("ot_dram", 2)
        b_w2bf = [buf(("w2bf", e)) for e in range(NE)]
        b_w13bf = [[buf(("w13bf", m, e)) for e in range(NE)] for m in range(2)]

        def xdram_buf(t, b):
            return buf(("xd", t.name, b))

        ph0 = Arena(nc, arena_base, ARENA)
        flat_rel, _ = ph0.place("flat_rel", [1, 16368], F32)
        ext_row, _ = ph0.place("ext_row", [1, 4096], F32)
        ext_bfr, _ = ph0.place("ext_bfr", [1, 4096], BF16)
        b_flat = buf("flat_rel")
        b_extr = buf("ext_row")
        b_extb = buf("ext_bfr")

        dma("G", irev[:, :], c_irev[:, :], writes=[b_const])
        op("V", lambda e: e.memset(epsc[:, :], EPS), writes=[b_small])
        dma("G", selb[:, :].rearrange("p (s m) -> p s m", s=2),
            c_sel[:, :, :].rearrange("s p m -> p s m"), writes=[b_const])
        dma("Y", biases[:, 0:8], abq[:, :], writes=[b_small])
        dma("Y", biases[:, 8:16], abk[:, :], writes=[b_small])
        dma("Y", biases[:, 16:28], bqkvbT[:, :], writes=[b_small])
        dma("G", brow[0:1, 0:D], abo[:].unsqueeze(0), writes=[b_small])
        dma("G", brow[0:1, D:2 * D], bob[:].unsqueeze(0), writes=[b_small])
        dma("G", brow[0:1, 2 * D:3 * D], abv[:].unsqueeze(0),
            writes=[b_small])
        dma("G", brow[0:1, 3 * D:3 * D + 256],
            bqkvb[1280:1536].unsqueeze(0), writes=[b_small])
        dma("Y", scol[:, 0:1], AP(bscal, 0, [[0, 2], [1, 64]]), writes=[b_small])
        op("S", lambda e: e.activation(scol[:, 1:2], scol[:, 0:1], AF.Abs),
           reads=[b_small], writes=[b_small])
        op("S", lambda e: e.activation(scol[:, 2:3], scol[:, 1:2], AF.Exp,
                                       scale=-1.0),
           reads=[b_small], writes=[b_small])
        op("S", lambda e: e.activation(scol[:, 2:3], scol[:, 2:3], AF.Ln, bias=1.0),
           reads=[b_small], writes=[b_small])
        op("V", lambda e: e.tensor_scalar_max(scol[:, 3:4], scol[:, 0:1], 0.0),
           reads=[b_small], writes=[b_small])
        op("V", lambda e: e.tensor_add(scol[:, 3:4], scol[:, 3:4], scol[:, 2:3]),
           reads=[b_small], writes=[b_small])
        op("V", lambda e: e.tensor_scalar_mul(
            scol[:, 3:4], scol[:, 3:4], float(R_SOFTPLUS_0 / np.sqrt(DK))),
           reads=[b_small], writes=[b_small])

        dma("Y", flat_rel[0:1, :], arel[:, :].rearrange("a b -> (a b)").unsqueeze(0),
            writes=[b_flat])
        for h in range(NH):
            op("V", lambda e, h=h: e.tensor_copy(
                ext_row[0:1, 0:1023],
                AP(flat_rel.tensor if hasattr(flat_rel, "tensor") else flat_rel,
                   h, [[16368, 1], [16, 1023]])
                if False else
                AP(flat_rel, h, [[16368, 1], [16, 1023]])),
               reads=[b_flat], writes=[b_extr])
            op("V", lambda e, h=h: e.tensor_copy(
                ext_row[0:1, 1023:4096],
                AP(flat_rel, 1022 * 16 + h, [[16368, 1], [0, 3073]])),
               reads=[b_flat], writes=[b_extr])
            op("S", lambda e: e.activation(ext_bfr[0:1, :], ext_row[0:1, :],
                                           AF.Copy, scale=float(1.0 / SCALE_A)),
               reads=[b_extr], writes=[b_extb])
            dma("Y", ext_dram[h, :].unsqueeze(0), ext_bfr[0:1, :],
                reads=[b_extb], writes=[b_ext])

        for r0 in range(0, 1536, 256):
            dma("G", w2cv[:, 0:2048].rearrange("p (c m) -> p c m", m=D),
                bqkvw[r0:r0 + 256, :].rearrange("(c p) m -> p c m", p=128),
                writes=[b_w2cv])
            dma("G", wTb_dram[r0:r0 + 256, :].rearrange("(c p) m -> p c m", p=128),
                w2cv[:, 0:2048].rearrange("p (c m) -> p c m", m=D),
                reads=[b_w2cv], writes=[buf("wTb_dram")])
        for r0 in range(0, D, 256):
            dma("G", w2cv[:, 0:2048].rearrange("p (c m) -> p c m", m=D),
                bow[r0:r0 + 256, :].rearrange("(c p) m -> p c m", p=128),
                writes=[b_w2cv])
            dma("G", wTo_dram[r0:r0 + 256, :].rearrange("(c p) m -> p c m", p=128),
                w2cv[:, 0:2048].rearrange("p (c m) -> p c m", m=D),
                reads=[b_w2cv], writes=[buf("wTo_dram")])

        phase0_bufs = [b_flat, b_extr, b_extb]

        def emit_ln(xsrc, b):
            for tb in range(TB):
                sl = tb % 2
                xt = x_tile[:, ts(sl, D)]
                dma("Y", xt, xsrc[b, ts(tb, 128), :],
                    reads=[xdram_buf(xsrc, b)], writes=[(b_xt, sl)])
                op("V", lambda e, xt=xt: e.bn_stats(stats[:, 0:6], xt[:, 0:512]),
                   reads=[(b_xt, sl)], writes=[b_stats])
                op("V", lambda e, xt=xt: e.bn_stats(stats[:, 6:12], xt[:, 512:1024]),
                   reads=[(b_xt, sl)], writes=[b_stats])
                op("V", lambda e: e.bn_aggr(mv[:, 0:2], stats[:, 0:12]),
                   reads=[b_stats], writes=[b_stats])
                op("S", lambda e: e.activation(mv[:, 2:3], mv[:, 1:2], AF.Ln,
                                               bias=epsc[:, 0:1]),
                   reads=[b_stats], writes=[b_stats])
                op("S", lambda e: e.activation(mv[:, 2:3], mv[:, 2:3], AF.Exp,
                                               scale=-0.5),
                   reads=[b_stats], writes=[b_stats])
                op("V", lambda e, xt=xt, sl=sl: e.tensor_scalar(
                    lnx_tile[:, ts(sl, D)], xt, mv[:, 0:1], mv[:, 2:3],
                    ALU.subtract, ALU.mult),
                   reads=[(b_xt, sl), b_stats], writes=[(b_lnxm, sl)])
                dma("Y", lnx_dram[b, ts(tb, 128), :], lnx_tile[:, ts(sl, D)],
                    reads=[(b_lnxm, sl)], writes=[b_lnd[b]])

        def emit_lnxT(dst, bdst, b):
            for c in range(8):
                dmaT("Y", dst[:, ts(c, S)], lnx_dram[b, :, ts(c, 128)],
                     reads=[b_lnd[b]], writes=[bdst])

        def emit_oproj(xsrc, xdst, otbuf, b_otb, wmov, b_wmov, bias_row, b_i):
            for tb in range(TB):
                sl = tb % 2
                ph = p_ot[:, ts(sl, 1024)]
                xt = x_tile[:, ts(sl, D)]
                dma("Y", xt, xsrc[b_i, ts(tb, 128), :],
                    reads=[xdram_buf(xsrc, b_i)], writes=[(b_xt, sl)])
                for n2 in range(2):
                    op("T", lambda e, ph=ph, n2=n2, bias_row=bias_row:
                       e.matmul(ph[:, ts(n2, 512)], selb[0:1, 0:128],
                                bias_row[0:1, ts(n2, 512)], start=True, stop=False),
                       reads=[b_const, b_small], writes=[(b_pot, sl)])
                for c in range(8):
                    for n2 in range(2):
                        op("T", lambda e, ph=ph, c=c, n2=n2, tb=tb, otbuf=otbuf,
                           wmov=wmov:
                           e.matmul(ph[:, ts(n2, 512)],
                                    otbuf[:, c * S + tb * 128:c * S + tb * 128 + 128],
                                    wmov[:, c * D + n2 * 512:c * D + n2 * 512 + 512],
                                    start=False, stop=(c == 7)),
                           reads=[b_otb, b_wmov], writes=[(b_pot, sl)])
                ot = o_tile[:, ts(sl, D)]
                op("V", lambda e, ot=ot, ph=ph, xt=xt: e.tensor_add(ot, ph, xt),
                   reads=[(b_pot, sl), (b_xt, sl)], writes=[(b_ot_t, sl)])
                dma("Y", xdst[b_i, ts(tb, 128), :], ot,
                    reads=[(b_ot_t, sl)], writes=[xdram_buf(xdst, b_i)],
                    force_inc=(xdst is x_out))

        PSC3 = (p_sc0[:, 0:1024], p_sc1[:, 0:1024], p_ot[:, 0:1024])
        B3 = (WHOLE0, WHOLE1, [(b_pot, 0)])
        p_pd = p_ot[:, 0:1024]
        B_PD = [(b_pot, 0)]
        p_otr = p_ot[:, 1024:2048]
        B_OTR = [(b_pot, 1)]

        def emit_attention(kind, xsrc, xdst):
            is_a = kind == "a"
            vw = (NH if is_a else 4) * 65
            ar = Arena(nc, arena_base, ARENA)
            lnxT, _ = ar.place("lnxT" + kind, [128, 8 * S], BF16)
            otbuf, _ = ar.place("otbuf" + kind, [128, 8 * S], BF16)
            vtok, _ = ar.place("vtok" + kind, [128, KB * vw], BF16)
            qkT, _ = ar.place("qkT" + kind, [128, 4 * S], BF16)
            probs, pr_at = ar.place("probs" + kind, [128, 3 * 1024], BF16)
            sqt, _ = ar.place("sqt" + kind, [128, S], BF16, at=pr_at)
            otraw, _ = ar.place("otraw" + kind, [65, S], BF16)
            denrep, dr_at = ar.place("denrep" + kind, [65, 2 * 1024], F32)
            if is_a:
                wpair, _ = ar.place("wpair_a", [128, 2 * 2 * 1024], BF16)
                rep2e, re_at = ar.place("rep2e_a", [128, 4 * 3968], BF16)
                wmov, _ = ar.place("wmov_a", [128, 8 * D], BF16, at=re_at)
                wv, _ = ar.place("wv_a", [128, 8 * D], BF16,
                                 at=re_at + 16 * 1024)
                wqkvT = kvT = kvTalt = None
            else:
                wqkvT, _ = ar.place("wqkvT", [128, 8 * 1536], BF16)
                kvT, _ = ar.place("kvT", [128, 2 * S], BF16)
                kvTalt, _ = ar.place("kvTalt", [128, 2 * S], BF16)
                wmov, _ = ar.place("wmov_b", [128, 8 * D], BF16)
                wpair = rep2e = wv = None

            nb = []

            def mk(name, nslots=1):
                b = Buf((kind, name), nslots)
                nb.append(b)
                return b

            b_lnxT = mk("lnxT")
            b_otb = mk("otbuf")
            b_vt = mk("vtok")
            b_qk = mk("qkT", 2)
            b_pr = mk("probs", 3)
            b_sq = mk("sqt")
            b_otr = mk("otraw")
            b_dr = mk("denrep", 2)
            b_wp = mk("wpair", 2)
            b_rep = mk("rep2e", 2)
            b_wmov = mk("wmov")
            b_kvT = mk("kvT")
            b_kvTa = mk("kvTalt")
            b_bd = mk("bnd", 2)
            b_bkv = mk("bndkv")
            b_kvd = buf("kv_dram")
            if is_a:
                b_wmov = b_rep
            yield nb

            emit_ln(xsrc, 0)
            emit_ln(xsrc, 1)

            if not is_a:
                for c in range(8):
                    dmaT("Y", wqkvT[:, ts(c, 1536)], wTb_dram[:, ts(c, 128)],
                         reads=[buf("wTb_dram")], writes=[b_wp])
                for c in range(8):
                    dmaT("Y", wmov[:, ts(c, D)], wTo_dram[:, ts(c, 128)],
                         reads=[buf("wTo_dram")], writes=[b_wmov])

            def proj_chunk_pd(ncc, dst_ap, b_dst, lhs_fn, lhs_reads, bias_col,
                              scale_v):
                half = ncc % 2
                pd = p_pd[:, ts(half, 512)]
                for c in range(8):
                    op("T", lambda e, c=c, ncc=ncc, pd=pd, lhs_fn=lhs_fn:
                       e.matmul(pd, lhs_fn(c),
                                lnxT[:, c * S + ncc * 512:c * S + ncc * 512 + 512],
                                start=(c == 0), stop=(c == 7)),
                       reads=list(lhs_reads) + [b_lnxT], writes=B_PD)
                if scale_v is None:
                    op("V", lambda e, ncc=ncc, pd=pd: e.tensor_scalar(
                        dst_ap[:, ts(ncc, 512)], pd, bias_col, None, ALU.add),
                       reads=B_PD + [b_small], writes=[b_dst])
                else:
                    op("V", lambda e, ncc=ncc, pd=pd: e.tensor_scalar(
                        dst_ap[:, ts(ncc, 512)], pd, bias_col, scale_v,
                        ALU.add, ALU.mult),
                       reads=B_PD + [b_small], writes=[b_dst])

            def bound_head(src_ap, b_src, half, col, b_b):
                op("V", lambda e: e.tensor_mul(sqt[:, :], src_ap, src_ap),
                   reads=[b_src], writes=[b_sq, (b_pr, 0), (b_pr, 1)])
                base12 = (col // 12) * 12
                for i2 in range(2):
                    for j2 in range(2):
                        op("T", lambda e, i2=i2, j2=j2:
                           e.matmul(p_pd[:, ts(j2, 512)], selb[:, ts(half, 128)],
                                    sqt[:, ts(2 * i2 + j2, 512)],
                                    start=True, stop=True),
                           reads=[b_sq, (b_pr, 0), (b_pr, 1), b_const],
                           writes=B_PD)
                    op("V", lambda e, i2=i2, base12=base12: e.tensor_reduce(
                        bnd[:, base12 + 8 + i2:base12 + 9 + i2],
                        p_pd[:, 0:1024], AXX, ALU.max),
                       reads=B_PD, writes=[b_b])
                op("V", lambda e, col=col, base12=base12: e.tensor_max(
                    bnd[:, col:col + 1], bnd[:, base12 + 8:base12 + 9],
                    bnd[:, base12 + 9:base12 + 10]),
                   reads=[b_b], writes=[b_b])

            def finish_bound(qcol, kcol, outcol, scale, b_b, extra_reads=()):
                tcol = (outcol // 12) * 12 + 10
                op("V", lambda e: e.tensor_mul(
                    bnd[:, tcol:tcol + 1], bnd[:, qcol:qcol + 1],
                    bnd[:, kcol:kcol + 1]),
                   reads=[b_b] + list(extra_reads), writes=[b_b])
                op("S", lambda e: e.activation(bnd[:, tcol:tcol + 1],
                                               bnd[:, tcol:tcol + 1], AF.Ln),
                   reads=[b_b], writes=[b_b])
                op("S", lambda e: e.activation(bnd[:, tcol:tcol + 1],
                                               bnd[:, tcol:tcol + 1], AF.Exp,
                                               scale=0.5),
                   reads=[b_b], writes=[b_b])
                op("V", lambda e: e.tensor_scalar_mul(
                    bnd[:, outcol:outcol + 1], bnd[:, tcol:tcol + 1],
                    -float(scale)),
                   reads=[b_b], writes=[b_b])

            exp_scale = SCALE_A if is_a else 1.0

            chainq = []

            def pump_chains(drain=False):
                i = 0
                while i < len(chainq):
                    g = chainq[i]
                    if next(g, None) is None:
                        chainq.pop(i)
                        continue
                    if not drain:
                        i += 1

            def prep_steps(pair, b_i):
                py = pair % 2
                qk0 = 2 * S * py
                if is_a:
                    for mi, wm in enumerate((aWq, aWk)):
                        dma("G",
                            wpair[:, (py * 2 + mi) * D:(py * 2 + mi + 1) * D]
                            .rearrange("p (c m) -> p c m", m=128),
                            wm[:, ts(pair, 128)].rearrange("(c p) m -> p c m",
                                                           p=128),
                            writes=[(b_wp, py)])
                    for hh in range(2):
                        h = 2 * pair + hh
                        dma("G", rep2e[:, (py * 2 + hh) * 3968:
                                       (py * 2 + hh + 1) * 3968],
                            AP(ext_dram, h * 4096, [[1, 128], [1, 3968]]),
                            reads=[b_ext], writes=[(b_rep, py)])
                    yield
                    for ncc in range(4):
                        proj_chunk_pd(ncc, qkT[:, qk0:qk0 + S], (b_qk, py),
                                      lambda c, py=py:
                                      wpair[:, (py * 2) * D + c * 128:
                                            (py * 2) * D + c * 128 + 128],
                                      [(b_wp, py)], biases[:, pair:pair + 1],
                                      None)
                        yield
                    for ncc in range(4):
                        proj_chunk_pd(ncc, qkT[:, qk0 + S:qk0 + 2 * S],
                                      (b_qk, py),
                                      lambda c, py=py:
                                      wpair[:, (py * 2 + 1) * D + c * 128:
                                            (py * 2 + 1) * D + c * 128 + 128],
                                      [(b_wp, py)], biases[:, 8 + pair:9 + pair],
                                      None)
                        yield
                    for hh in range(2):
                        bound_head(qkT[:, qk0:qk0 + S], (b_qk, py), hh,
                                   py * 12 + 0 + hh, (b_bd, py))
                        yield
                        bound_head(qkT[:, qk0 + S:qk0 + 2 * S], (b_qk, py), hh,
                                   py * 12 + 2 + hh, (b_bd, py))
                        yield
                        finish_bound(py * 12 + 0 + hh, py * 12 + 2 + hh,
                                     py * 12 + 6 + hh, SCALE_A, (b_bd, py))
                        yield
                else:
                    for ncc in range(4):
                        proj_chunk_pd(ncc, qkT[:, qk0:qk0 + S], (b_qk, py),
                                      lambda c, pair=pair:
                                      wqkvT[:, c * 1536 + pair * 128:
                                            c * 1536 + pair * 128 + 128],
                                      [b_wp], biases[:, 16 + pair:17 + pair],
                                      scol[:, 3:4])
                        yield
                    for hh in range(2):
                        g = (2 * pair + hh) // 4
                        bound_head(qkT[:, qk0:qk0 + S], (b_qk, py), hh,
                                   py * 12 + 0 + hh, (b_bd, py))
                        yield
                        finish_bound(py * 12 + 0 + hh, 24 + g,
                                     py * 12 + 6 + hh, 1.0, (b_bd, py),
                                     extra_reads=[b_bkv])
                        yield
                op("V", lambda e: e.tensor_tensor(
                    bnd[:, py * 12 + 11:py * 12 + 12],
                    bnd[:, py * 12 + 6:py * 12 + 7],
                    bnd[:, py * 12 + 7:py * 12 + 8], ALU.min),
                   reads=[(b_bd, py)], writes=[(b_bd, py)])

            def sdpa_steps(pair, b_i):
                py = pair % 2
                qk0 = 2 * S * py
                h0 = 2 * pair
                HH = []
                for hh in range(2):
                    h = h0 + hh
                    if is_a:
                        HH.append((qkT[:, qk0 + S:qk0 + 2 * S], (b_qk, py),
                                   h * 65, (py * 2 + hh) * 3968))
                    else:
                        g = h // 4
                        if (g % 2) == (hh % 2):
                            HH.append((kvT[:, ts(g // 2, S)], b_kvT, g * 65,
                                       None))
                        else:
                            HH.append((kvTalt[:, ts(g // 2, S)], b_kvTa,
                                       g * 65, None))
                negb = bnd[:, py * 12 + 11:py * 12 + 12]

                for q4 in range(4):
                    half = q4 // 2
                    qoff = q4 * 512

                    def emit_sc(kb, q4=q4, qoff=qoff):
                        sl = kb % 3
                        pt = PSC3[sl]
                        pp = B3[sl]
                        for hh in range(2):
                            kT_ap, b_k, voff, repoff = HH[hh]
                            base = 64 * hh
                            op("T", lambda e, pt=pt, kb=kb, hh=hh, base=base,
                               kT_ap=kT_ap, b_k=b_k:
                               e.matmul(
                                   pt[:, ts(hh, 512)],
                                   kT_ap[base:base + 64, ts(kb, 128)],
                                   qkT[base:base + 64, qk0 + qoff:
                                       qk0 + qoff + 512],
                                   start=True, stop=not is_a),
                               reads=[b_k, (b_qk, py)], writes=pp)
                        if is_a:
                            for hh in range(2):
                                repoff = HH[hh][3]
                                off = repoff + 1920 - 128 * kb + qoff
                                op("T", lambda e, pt=pt, hh=hh, off=off:
                                   e.matmul(
                                       pt[:, ts(hh, 512)], irev[:, :],
                                       rep2e[:, off:off + 512],
                                       start=False, stop=True),
                                   reads=[(b_rep, py), b_const], writes=pp)

                    def emit_exp(kb, q4=q4, qoff=qoff):
                        sl = kb % 3
                        pt = PSC3[sl]
                        pp = B3[sl]
                        op("S", lambda e, pt=pt, sl=sl:
                           e.activation(probs[:, ts(sl, 1024)], pt[:, 0:1024],
                                        AF.Exp, bias=negb,
                                        scale=float(exp_scale)),
                           reads=pp + [(b_bd, py)], writes=[(b_pr, sl)])

                    def emit_pv(kb, q4=q4):
                        sl = kb % 3
                        for hh in range(2):
                            voff = HH[hh][2]
                            op("T", lambda e, kb=kb, sl=sl, hh=hh, voff=voff:
                               e.matmul(
                                   p_otr[0:65, ts(hh, 512)],
                                   vtok[:, kb * vw + voff:kb * vw + voff + 65],
                                   probs[:, sl * 1024 + hh * 512:
                                         sl * 1024 + hh * 512 + 512],
                                   start=(kb == 0), stop=(kb == KB - 1)),
                               reads=[(b_pr, sl), b_vt], writes=B_OTR)

                    emit_sc(0)
                    emit_sc(1)
                    for kb in range(KB):
                        emit_exp(kb)
                        if kb + 2 < KB:
                            emit_sc(kb + 2)
                        emit_pv(kb)
                        if kb % 2 == 1:
                            yield False

                    pump_chains()
                    for hh in range(2):
                        even = hh == 0
                        if even:
                            dst_q = otbuf[0:64, pair * S + qoff:
                                          pair * S + qoff + 512]
                            b_dst = b_otb
                        else:
                            dst_q = otraw[0:64, qoff:qoff + 512]
                            b_dst = b_otr
                        op("V", lambda e, dst_q=dst_q, hh=hh: e.tensor_copy(
                            dst_q, p_otr[0:64, ts(hh, 512)]),
                           reads=B_OTR, writes=[b_dst])
                        op("V", lambda e, hh=hh, q4=q4: e.tensor_copy(
                            denrep[64:65, hh * 1024 + (q4 % 2) * 512:
                                   hh * 1024 + (q4 % 2) * 512 + 512],
                            p_otr[64:65, ts(hh, 512)]),
                           reads=B_OTR, writes=[(b_dr, hh)])
                        if q4 % 2 == 1:
                            op_half = half

                            def chain(hh=hh, op_half=op_half, even=even,
                                      pair=pair):
                                dma("Y", den_dram[hh, ts(op_half, 1024)]
                                    .unsqueeze(0),
                                    denrep[64:65, ts(hh, 1024)],
                                    reads=[(b_dr, hh)], writes=[(b_dend, hh)])
                                dma("Y", rsq[:, hh * 32 + op_half * 8:
                                             hh * 32 + op_half * 8 + 8],
                                    den_dram[hh, ts(op_half, 1024)]
                                    .rearrange("(p c) -> p c", c=8),
                                    reads=[(b_dend, hh)], writes=[(b_rsq, hh)])
                                yield 1
                                op("V", lambda e: e.reciprocal(
                                    rsq[:, hh * 32 + 16 + op_half * 8:
                                        hh * 32 + 16 + op_half * 8 + 8],
                                    rsq[:, hh * 32 + op_half * 8:
                                        hh * 32 + op_half * 8 + 8]),
                                   reads=[(b_rsq, hh)], writes=[(b_rsq, hh)])
                                dma("Y", rec_dram[hh, ts(op_half, 1024)]
                                    .rearrange("(p c) -> p c", c=8),
                                    rsq[:, hh * 32 + 16 + op_half * 8:
                                        hh * 32 + 16 + op_half * 8 + 8],
                                    reads=[(b_rsq, hh)], writes=[(b_recd, hh)])
                                dma("Y", denrep[0:64, ts(hh, 1024)],
                                    AP(rec_dram, hh * S + op_half * 1024,
                                       [[0, 64], [1, 1024]]),
                                    reads=[(b_recd, hh)], writes=[(b_dr, hh)])
                                yield 1
                                if even:
                                    dst_h = otbuf[0:64,
                                                  pair * S + op_half * 1024:
                                                  pair * S + op_half * 1024
                                                  + 1024]
                                    b_dsth = b_otb
                                else:
                                    dst_h = otraw[0:64, ts(op_half, 1024)]
                                    b_dsth = b_otr
                                op("V", lambda e: e.tensor_mul(
                                    dst_h, dst_h, denrep[0:64, ts(hh, 1024)]),
                                   reads=[(b_dr, hh), b_dsth], writes=[b_dsth])
                                if not even:
                                    dma("G", ot_dram[hh, :, ts(op_half, 1024)],
                                        otraw[0:64, ts(op_half, 1024)],
                                        reads=[b_dsth], writes=[(b_otd, hh)])
                                    dma("G", otbuf[64:128,
                                                   pair * S + op_half * 1024:
                                                   pair * S + op_half * 1024
                                                   + 1024],
                                        ot_dram[hh, :, ts(op_half, 1024)],
                                        reads=[(b_otd, hh)], writes=[b_otb])

                            g = chain()
                            next(g)
                            chainq.append(g)
                    yield True

            ones_done = [False]
            for b_i in range(NB):
                emit_lnxT(lnxT, b_lnxT, b_i)

                if is_a:
                    dma("G", wv[:, :].rearrange("p (c m) -> p c m", m=D),
                        aWv[:, :].rearrange("(c p) m -> p c m", p=128),
                        writes=[(b_rep, 0), (b_rep, 1)])
                    if not ones_done[0]:
                        op("V", lambda e: e.memset(
                            vtok[:, :].rearrange("p (t h o) -> p t h o",
                                                 h=NH, o=65)
                            [:, :, :, 64:65], 1.0), writes=[b_vt])
                        ones_done[0] = True
                    for tb in range(TB):
                        pp = WHOLE0 if tb % 2 == 0 else WHOLE1
                        pt = PSC[tb % 2]
                        for n2 in range(2):
                            op("T", lambda e, pt=pt, n2=n2: e.matmul(
                                pt[:, ts(n2, 512)], selb[0:1, 0:128],
                                brow[0:1, 2 * D + n2 * 512:2 * D + n2 * 512 + 512],
                                start=True, stop=False),
                               reads=[b_const, b_small], writes=pp)
                            for c in range(8):
                                op("T", lambda e, tb=tb, c=c, n2=n2, pt=pt:
                                   e.matmul(
                                       pt[:, ts(n2, 512)],
                                       lnxT[:, c * S + tb * 128:
                                            c * S + tb * 128 + 128],
                                       wv[:, c * D + n2 * 512:
                                          c * D + n2 * 512 + 512],
                                       start=False, stop=(c == 7)),
                                   reads=[(b_rep, 0), (b_rep, 1), b_lnxT],
                                   writes=pp)
                        op("V", lambda e, tb=tb, pt=pt: e.tensor_copy(
                            vtok[:, tb * vw:(tb + 1) * vw]
                            .rearrange("p (h o) -> p h o", o=65)[:, :, 0:64],
                            pt[:, 0:1024].rearrange("p (h o) -> p h o", o=64)),
                           reads=pp, writes=[b_vt])
                else:
                    for blk2 in range(2):
                        for ncc in range(4):
                            proj_chunk_pd(ncc, kvT[:, ts(blk2, S)], b_kvT,
                                          lambda c, blk2=blk2:
                                          wqkvT[:, c * 1536 + 1024 + blk2 * 128:
                                                c * 1536 + 1024 + blk2 * 128
                                                + 128],
                                          [b_wp],
                                          biases[:, 24 + blk2:25 + blk2], None)
                    dma("Y", kv_dram[:, :], kvT[:, 0:2 * S],
                        reads=[b_kvT], writes=[b_kvd])
                    for blk2 in range(2):
                        dma("Y", kvTalt[0:64, ts(blk2, S)],
                            kv_dram[64:128, ts(blk2, S)],
                            reads=[b_kvd], writes=[b_kvTa])
                        dma("Y", kvTalt[64:128, ts(blk2, S)],
                            kv_dram[0:64, ts(blk2, S)],
                            reads=[b_kvd], writes=[b_kvTa])
                    if not ones_done[0]:
                        op("V", lambda e: e.memset(
                            vtok[:, :].rearrange("p (t h o) -> p t h o",
                                                 h=4, o=65)
                            [:, :, :, 64:65], 1.0), writes=[b_vt])
                        ones_done[0] = True
                    for tb in range(TB):
                        pp = WHOLE0 if tb % 2 == 0 else WHOLE1
                        pt = PSC[tb % 2]
                        op("T", lambda e, pt=pt: e.matmul(
                            pt[:, 0:256], selb[0:1, 0:128],
                            brow[0:1, 3 * D:3 * D + 256], start=True,
                            stop=False),
                           reads=[b_const, b_small], writes=pp)
                        for c in range(8):
                            op("T", lambda e, tb=tb, c=c, pt=pt: e.matmul(
                                pt[:, 0:256],
                                lnxT[:, c * S + tb * 128:c * S + tb * 128 + 128],
                                wqkvT[:, c * 1536 + 1280:c * 1536 + 1536],
                                start=False, stop=(c == 7)),
                               reads=[b_wp, b_lnxT], writes=pp)
                        for g in range(4):
                            op("V", lambda e, tb=tb, g=g, pt=pt: e.tensor_copy(
                                vtok[:, tb * vw + g * 65:tb * vw + g * 65 + 64],
                                pt[:, g * 64:g * 64 + 64]),
                               reads=pp, writes=[b_vt])
                    for g in range(4):
                        bound_head(kvT[:, ts(g // 2, S)], b_kvT, g % 2,
                                   24 + g, b_bkv)

                for _ in prep_steps(0, b_i):
                    pass
                for pair in range(8):
                    g_s = sdpa_steps(pair, b_i)
                    g_p = prep_steps(pair + 1, b_i) if pair < 7 else iter(())
                    for safe in g_s:
                        if safe:
                            for _ in range(4):
                                next(g_p, None)
                    for _ in g_p:
                        pass

                pump_chains(drain=True)
                if is_a:
                    dma("G", wmov[:, :].rearrange("p (c m) -> p c m", m=D),
                        aWo[:, :].rearrange("(c p) m -> p c m", p=128),
                        writes=[(b_rep, 0), (b_rep, 1)])
                bias_row = brow[0:1, 0:D] if is_a else brow[0:1, D:2 * D]
                emit_oproj(xsrc, xdst, otbuf, b_otb, wmov, b_wmov, bias_row, b_i)

        def emit_ffn(xsrc, xdst, call_i):
            ar = Arena(nc, arena_base, ARENA)
            lnxTf, _ = ar.place("f_lnxT", [128, 8 * 2 * S], BF16)
            h_buf, _ = ar.place("f_h", [128, 22 * 512], BF16)
            w13, _ = ar.place("f_w13", [128, 4 * 8 * 768], BF16)
            w2s, _ = ar.place("f_w2", [128, 2 * 8 * 1024], BF16)
            sg, _ = ar.place("f_sg", [128, 2 * 512], BF16)

            nb = []

            def mk(name, nslots=1):
                b = Buf(("f", name), nslots)
                nb.append(b)
                return b

            b_lnxT = mk("lnxTf", NE)
            b_h = mk("h", 22)
            b_w13 = mk("w13", 4)
            b_w2 = mk("w2", 2)
            b_sg = mk("sg", 2)
            yield nb

            emit_ln(xsrc, 0)
            emit_ln(xsrc, 1)
            for e_i in range(NE):
                for b_i in range(NB):
                    for c in range(8):
                        dmaT("Y", lnxTf[:, c * 2 * S + e_i * 512 + b_i * 256:
                                        c * 2 * S + e_i * 512 + b_i * 256 + 256],
                             lnx_dram[b_i, ts(e_i, 256), ts(c, 128)],
                             reads=[b_lnd[b_i]], writes=[(b_lnxT, e_i)])

            quarters = [(0, 6), (6, 6), (12, 5), (17, 5)]
            w2thirds = [(0, 8), (8, 7), (15, 7)]

            def w13_load(e_i, q_i):
                mb0, mbn = quarters[q_i]
                for mat_i in range(2):
                    wd = (w1bf_dram, w3bf_dram)[mat_i]
                    slot = mat_i * 2 + (e_i * 4 + q_i) % 2
                    view = (w13[:, slot * 8 * 768:slot * 8 * 768 + 8 * mbn * 128]
                            .rearrange("p (c m) -> p c m", m=mbn * 128))
                    dview = (wd[e_i, :, mb0 * 128:(mb0 + mbn) * 128]
                             .rearrange("(c p) m -> p c m", p=128))
                    if call_i == 0:
                        wm = (fw1, fw3)[mat_i]
                        dma("G", view,
                            wm[e_i, :, mb0 * 128:(mb0 + mbn) * 128]
                            .rearrange("(c p) m -> p c m", p=128),
                            writes=[(b_w13, slot)])
                        dma("Y", dview, view,
                            reads=[(b_w13, slot)], writes=[b_w13bf[mat_i][e_i]])
                    else:
                        dma("Y", view, dview,
                            reads=[b_w13bf[mat_i][e_i]], writes=[(b_w13, slot)])

            def w2_load(e_i, t_i):
                kc0, kcn = w2thirds[t_i]
                slot = (e_i * 3 + t_i) % 2
                view = (w2s[:, slot * 8 * D:slot * 8 * D + kcn * D]
                        .rearrange("p (c m) -> p c m", m=D))
                dview = (w2bf_dram[e_i, slice(kc0 * 128, (kc0 + kcn) * 128), :]
                         .rearrange("(c p) m -> p c m", p=128))
                if call_i == 0:
                    dma("G", view,
                        fw2[e_i, slice(kc0 * 128, (kc0 + kcn) * 128), :]
                        .rearrange("(c p) m -> p c m", p=128),
                        writes=[(b_w2, slot)])
                    dma("Y", dview, view,
                        reads=[(b_w2, slot)], writes=[b_w2bf[e_i]])
                else:
                    dma("Y", view, dview,
                        reads=[b_w2bf[e_i]], writes=[(b_w2, slot)])

            for e_i in range(NE):
                for q_i, (mb0, mbn) in enumerate(quarters):
                    w13_load(e_i, q_i)
                    slot1 = 0 * 2 + (e_i * 4 + q_i) % 2
                    slot3 = 1 * 2 + (e_i * 4 + q_i) % 2
                    for mbl in range(mbn):
                        mb = mb0 + mbl
                        gsl = mb % 2
                        for gi, slot in ((0, slot1), (1, slot3)):
                            pt = PSC[gi]
                            pp = (b_psc[gi], gsl)
                            for c in range(8):
                                op("T", lambda e, pt=pt, gsl=gsl, c=c,
                                   slot=slot, mbl=mbl, mbn=mbn, e_i=e_i:
                                   e.matmul(
                                       pt[:, ts(gsl, 512)],
                                       w13[:, slot * 8 * 768 + (c * mbn + mbl)
                                           * 128:
                                           slot * 8 * 768 + (c * mbn + mbl)
                                           * 128 + 128],
                                       lnxTf[:, c * 2 * S + e_i * 512:
                                             c * 2 * S + e_i * 512 + 512],
                                       start=(c == 0), stop=(c == 7)),
                                   reads=[(b_w13, slot), (b_lnxT, e_i)],
                                   writes=[pp])
                        op("S", lambda e, gsl=gsl: e.activation(
                            sg[:, ts(gsl, 512)], p_sc0[:, ts(gsl, 512)], AF.Silu),
                           reads=[(b_psc[0], gsl)], writes=[(b_sg, gsl)])
                        op("V", lambda e, mb=mb, gsl=gsl: e.scalar_tensor_tensor(
                            h_buf[:, ts(mb, 512)], p_sc1[:, ts(gsl, 512)], 0.0,
                            sg[:, ts(gsl, 512)], ALU.add, ALU.mult),
                           reads=[(b_psc[1], gsl), (b_sg, gsl)],
                           writes=[(b_h, mb)])

                PH = [p_ot[:, 0:1024], p_ot[:, 1024:2048],
                      p_sc0[:, 0:1024], p_sc1[:, 0:1024]]
                PHB = [[(b_pot, 0)], [(b_pot, 1)], WHOLE0, WHOLE1]
                for t_i, (kc0, kcn) in enumerate(w2thirds):
                    w2_load(e_i, t_i)
                    slot = (e_i * 3 + t_i) % 2
                    for kcl in range(kcn):
                        kc = kc0 + kcl
                        for t4 in range(4):
                            for n2 in range(2):
                                op("T", lambda e, kc=kc, t4=t4, n2=n2, slot=slot,
                                   kcl=kcl:
                                   e.matmul(
                                       PH[t4][:, ts(n2, 512)],
                                       h_buf[:, kc * 512 + t4 * 128:
                                             kc * 512 + t4 * 128 + 128],
                                       w2s[:, slot * 8 * D + kcl * D + n2 * 512:
                                           slot * 8 * D + kcl * D + n2 * 512
                                           + 512],
                                       start=(kc == 0), stop=(kc == 21)),
                                   reads=[(b_h, kc), (b_w2, slot)],
                                   writes=PHB[t4])
                for t4 in range(4):
                    b_i = t4 // 2
                    tok0 = e_i * 256 + (t4 % 2) * 128
                    sl = t4 % 2
                    xt = x_tile[:, ts(sl, D)]
                    dma("Y", xt, xsrc[b_i, tok0:tok0 + 128, :],
                        reads=[xdram_buf(xsrc, b_i)], writes=[(b_xt, sl)])
                    ot = o_tile[:, ts(sl, D)]
                    op("V", lambda e, ot=ot, t4=t4, xt=xt: e.tensor_add(
                        ot, PH[t4][:, 0:1024], xt),
                       reads=PHB[t4] + [(b_xt, sl)], writes=[(b_ot_t, sl)])
                    dma("Y", xdst[b_i, tok0:tok0 + 128, :], ot,
                        reads=[(b_ot_t, sl)], writes=[xdram_buf(xdst, b_i)],
                        force_inc=(xdst is x_out))

        def run_phase(gen, retire_tokens):
            newbufs = next(gen)
            Sched.fence_into(newbufs, retire_tokens)
            try:
                next(gen)
            except StopIteration:
                pass
            return Sched.retire(newbufs)

        chain = [x_in, xA, xB, xA]
        outs = [xA, xB, xA, x_out]
        outs[n_sub - 1] = x_out

        retire_tokens = Sched.retire(phase0_bufs)
        for si in range(n_sub):
            if si in (0, 2):
                g = emit_attention("a" if si == 0 else "b", chain[si], outs[si])
            else:
                g = emit_ffn(chain[si], outs[si], 0 if si == 1 else 1)
            retire_tokens = run_phase(g, retire_tokens)

        bo = [xdram_buf(x_out, b) for b in range(NB)]
        op("Y", lambda e: e.drain(), reads=bo, force_inc=True)

        S_.emit(es)
    return nc



_CACHE = {}


def _get_program(n_sub=4):
    if n_sub not in _CACHE:
        _CACHE[n_sub] = build_program(n_sub)
    return _CACHE[n_sub]


def kernel(**inputs):
    nc = _get_program(4)
    sel = np.zeros((2, 128, 128), dtype=np.float32)
    sel[0, 0:64, :] = 1.0
    sel[1, 64:128, :] = 1.0
    irev = np.eye(128, dtype=np.float32)[::-1].copy()
    common = {
        "aWq": inputs["L0_attn_a_Wq"], "aWk": inputs["L0_attn_a_Wk"],
        "aWv": inputs["L0_attn_a_Wv"], "aWo": inputs["L0_attn_a_Wo"],
        "abq": inputs["L0_attn_a_bq"].reshape(8, 128).T,
        "abk": inputs["L0_attn_a_bk"].reshape(8, 128).T,
        "abv": inputs["L0_attn_a_bv"], "abo": inputs["L0_attn_a_bo"],
        "arel": inputs["L0_attn_a_rel_table"],
        "bscal": inputs["L0_attn_b_scaling"],
        "bqkvw": inputs["L0_attn_b_qkv_w"], "bqkvb": inputs["L0_attn_b_qkv_b"],
        "bqkvbT": inputs["L0_attn_b_qkv_b"].reshape(12, 128).T,
        "bow": inputs["L0_attn_b_o_w"], "bob": inputs["L0_attn_b_o_b"],
        "fw1": inputs["L0_ffn_w1"], "fw3": inputs["L0_ffn_w3"],
        "fw2": inputs["L0_ffn_w2"],
        "c_irev": irev, "c_sel": sel,
    }
    common = {k: np.ascontiguousarray(v, dtype=np.float32)
              for k, v in common.items()}
    x = np.ascontiguousarray(inputs["L0_attn_a_x"], dtype=np.float32)
    in_maps = []
    for c in range(8):
        m = dict(common)
        m["x"] = x[c * NB:(c + 1) * NB]
        in_maps.append(m)
    res = run_bass_kernel_spmd(nc, in_maps, core_ids=list(range(8)))
    out = np.concatenate([r["out"] for r in res.results], axis=0)
    return out
